# version 03
# baseline (speedup 1.0000x reference)
import numpy as np
import ml_dtypes
import concourse.bass as bass
import concourse.mybir as mybir
from concourse.bass_utils import run_bass_kernel_spmd

F32 = mybir.dt.float32
BF16 = mybir.dt.bfloat16
AF = mybir.ActivationFunctionType
ALU = mybir.AluOpType
AX = mybir.AxisListType

D = 1024
S = 4096
NCH = 8
NL = 6
LW = 512
WA = 512
WF = 1024
HEADS = 16
DH = 64
WIN = 256
DI2 = 5632
NPAIR = 22
DIP = 2816
NEG = -1.0e30

PV_BQ = 0
PV_BK = 8
PV_BO = 16
PV_L0 = 24
PV_PER_L = 68
PV_LNG = 0
PV_LNB = 8
PV_B2 = 16
PV_B1 = 24
NPV = PV_L0 + NL * PV_PER_L


def _host_tables():
    half = np.arange(0, DH, 2, dtype=np.float32)
    inv_freq = (1.0 / 10000.0 ** (half / DH)).astype(np.float32)
    t = np.arange(S, dtype=np.float32)
    freqs = t[:, None] * inv_freq[None, :]
    freqs = np.concatenate([freqs, freqs], -1)
    base_scale = ((half + 0.4 * DH) / (1.4 * DH)).astype(np.float32)
    power = (t - S // 2) / (WIN // 2)
    scale = base_scale[None, :] ** power[:, None]
    scale = np.concatenate([scale, scale], -1).astype(np.float32)
    cos = np.cos(freqs).astype(np.float32)
    sin = np.sin(freqs).astype(np.float32)
    inv = (1.0 / scale).astype(np.float32)
    sign = np.concatenate([-np.ones(DH // 2, np.float32), np.ones(DH // 2, np.float32)])
    t1q = cos * scale
    t2q = sign[None, :] * sin * scale
    t1k = cos * inv
    t2k = sign[None, :] * sin * inv
    return (np.ascontiguousarray(a) for a in (t1q, t2q, t1k, t2k))


def _host_masks():
    m = np.zeros((3, 128, 384), np.float32)
    p = np.arange(128)[:, None]
    c = np.arange(384)[None, :]
    allowed = (c >= p) & (c <= p + 256)
    for v, cut in ((0, 256), (1, 128), (2, 0)):
        a = allowed & (c >= cut)
        m[v] = np.where(a, 0.0, NEG)
    return m


def _pack_pvec(ins):
    pv = np.zeros((128, NPV), np.float32)

    def put(col0, vec):
        v = np.asarray(vec, np.float32).reshape(-1)
        n = v.shape[0] // 128
        pv[:, col0:col0 + n] = v.reshape(n, 128).T

    put(PV_BQ, ins["L0_attn_bq"])
    put(PV_BK, ins["L0_attn_bk"])
    put(PV_BO, ins["L0_attn_bo"])
    for l in range(NL):
        base = PV_L0 + l * PV_PER_L
        put(base + PV_LNG, ins["L0_ffn_ln_g"][l])
        put(base + PV_LNB, ins["L0_ffn_ln_b"][l])
        put(base + PV_B2, ins["L0_ffn_b2"][l])
        b1 = np.asarray(ins["L0_ffn_b1"][l], np.float32)
        b1p = _permute_w1_cols(b1[None, :])[0]
        put(base + PV_B1, b1p)
    return pv


def _permute_w1_cols(w1):
    lead = w1.shape[:-1]
    a = w1[..., :2730]
    g = w1[..., 2730:]
    out = np.zeros(lead + (DI2,), w1.dtype)
    for i in range(NPAIR):
        lo, hi = 128 * i, min(128 * (i + 1), 2730)
        n = hi - lo
        out[..., 256 * i:256 * i + n] = a[..., lo:hi]
        out[..., 256 * i + 128:256 * i + 128 + n] = g[..., lo:hi]
    return out


def _pad_w2_rows(w2):
    out = np.zeros((w2.shape[0], DIP, w2.shape[2]), w2.dtype)
    out[:, :2730, :] = w2
    return out


_HOST_CACHE = {}


def host_shared(ins):
    key = id(ins.get("L0_attn_Wq"))
    t1q, t2q, t1k, t2k = _host_tables()
    sh = {
        "wqT": np.ascontiguousarray(np.asarray(ins["L0_attn_Wq"], np.float32).T),
        "wkT": np.ascontiguousarray(np.asarray(ins["L0_attn_Wk"], np.float32).T),
        "wvT": np.ascontiguousarray(np.asarray(ins["L0_attn_Wv"], np.float32).T),
        "woT": np.ascontiguousarray(np.asarray(ins["L0_attn_Wo"], np.float32).T),
        "wqkv": np.ascontiguousarray(np.asarray(ins["L0_ffn_Wqkv"], np.float32)),
        "wom": np.ascontiguousarray(np.asarray(ins["L0_ffn_Wo"], np.float32)),
        "w1p": np.ascontiguousarray(_permute_w1_cols(np.asarray(ins["L0_ffn_W1"], np.float32))),
        "w2p": np.ascontiguousarray(_pad_w2_rows(np.asarray(ins["L0_ffn_W2"], np.float32))),
        "pvec": _pack_pvec(ins),
        "brow": np.asarray(ins["L0_attn_bv"], np.float32).reshape(1, D),
        "qsks": np.concatenate(
            [np.asarray(ins["L0_ffn_q_scale"], np.float32),
             np.asarray(ins["L0_ffn_k_scale"], np.float32)], axis=1),
        "t1q": t1q, "t2q": t2q, "t1k": t1k, "t2k": t2k,
        "lmask": _host_masks(),
    }
    return sh


def dch(ap2d, p=128):
    return ap2d.rearrange("(c p) n -> p c n", p=p)


def sch(ap2d, c):
    return ap2d.rearrange("p (c n) -> p c n", c=c)


class Ctr:
    def __init__(self, h):
        self.h = h
        self.n = 0

    def inc(self, instr, k=1):
        instr.then_inc(self.h, k)
        self.n += k


class KB:
    def __init__(self, nc, debug=()):
        self.nc = nc
        self.debug = set(debug)
        self.dbg_outs = {}
        n = nc

        def din(name, shape, dt=F32):
            return n.dram_tensor(name, shape, dt, kind="ExternalInput")

        self.xT = din("xT", [D, S])
        self.wqT = din("wqT", [D, D]); self.wkT = din("wkT", [D, D])
        self.wvT = din("wvT", [D, D]); self.woT = din("woT", [D, D])
        self.wqkv = din("wqkv", [NL, D, 3 * D])
        self.wom = din("wom", [NL, D, D])
        self.w1p = din("w1p", [NL, D, DI2])
        self.w2p = din("w2p", [NL, DIP, D])
        self.pvec_d = din("pvec", [128, NPV])
        self.brow_d = din("brow", [1, D])
        self.qsks_d = din("qsks", [NL, 128])
        self.t1q_d = din("t1q", [S, DH]); self.t2q_d = din("t2q", [S, DH])
        self.t1k_d = din("t1k", [S, DH]); self.t2k_d = din("t2k", [S, DH])
        self.lmask_d = din("lmask", [3, 128, 384])

        self.outT = n.dram_tensor("outT", [D, S], F32, kind="ExternalOutput")

        def sc(name, shape, dt):
            return n.dram_tensor(name, shape, dt, kind="Internal")

        self.xn_b = sc("xn_b", [D, S], BF16)
        self.xmA = sc("xmA", [D, S], F32)
        self.xmB = sc("xmB", [D, S], F32)
        self.vsc = sc("vsc", [S, D], BF16)
        self.ssc = sc("ssc", [D, S], F32)
        self.wqT_b = sc("wqT_b", [D, D], BF16); self.wkT_b = sc("wkT_b", [D, D], BF16)
        self.wvT_b = sc("wvT_b", [D, D], BF16); self.woT_b = sc("woT_b", [D, D], BF16)
        self.wqkv_b = sc("wqkv_b", [NL, D, 3 * D], BF16)
        self.wom_b = sc("wom_b", [NL, D, D], BF16)
        self.w1_b = sc("w1_b", [NL, D, DI2], BF16)
        self.w2_b = sc("w2_b", [NL, DIP, D], BF16)
        self.stat_sc = sc("stat_sc", [64, 2], F32)

        ident_np = np.eye(128).astype(ml_dtypes.bfloat16)
        self.ident_d = n.inline_tensor(ident_np, "ident")

        self._es_global = []
        self.sD = Ctr(self._sem("sD"))
        self.sO = Ctr(self._sem("sO"))
        self.sP = Ctr(self._sem("sP"))
        self.sV = Ctr(self._sem("sV"))
        self.sA = Ctr(self._sem("sA"))
        self.sG = Ctr(self._sem("sG"))

        self.pvec = self._sb("pvec", [128, NPV], F32)
        self.brow = self._sb("brow", [1, D], F32)
        self.brow_bf = self._sb("brow_bf", [1, D], BF16)
        self.qsks = self._sb("qsks", [NL, 128], F32)
        self.ident = self._sb("ident", [128, 128], BF16)
        self.masks = self._sb("masks", [128, 3 * 384], F32)
        self.ones_k = self._sb("ones_k", [128, 1], BF16)
        self.ones_m = self._sb("ones_m", [1, 128], BF16)
        self.ones_mf = self._sb("ones_mf", [1, 128], F32)
        self.negc128 = self._sb("negc128", [128, NL], F32)
        self.eps24 = self._sb("eps24", [128, 1], F32)
        self.eps5 = self._sb("eps5", [1, 1], F32)
        self.tiny = self._sb("tinybuf", [128, 64], F32)
        self.snap4 = self._sb("snap4", [128, 320], F32)

    def _sem(self, name):
        cm = self.nc.semaphore(name)
        h = cm.__enter__()
        self._es_global.append(cm)
        return h

    def _sb(self, name, shape, dt):
        return self.nc.alloc_sbuf_tensor("sb_" + name, shape, dt)

    def tap5(self, blk, tag, buf, shape, dt):
        name = "dbg_" + tag
        if name in self.dbg_outs:
            return
        t = self.nc.dram_tensor(name, shape, dt, kind="ExternalOutput")
        self.dbg_outs[name] = t

        def f(e):
            e.wait_ge(self.sV.h, self.sV.n)
            e.wait_ge(self.sA.h, self.sA.n)
            e.wait_ge(self.sP.h, self.sP.n)
            e.dma_start(t.ap()[:, :], buf[:, 0:shape[1]]).then_inc(self.sO.h, 16)
        self.sO.n += 16
        blk.gpsimd(f)

    def tap(self, blk, tag, ap_fn, shape, dt=F32):
        if tag not in self.debug:
            return
        name = "dbg_" + tag
        if name not in self.dbg_outs:
            self.dbg_outs[name] = self.nc.dram_tensor(name, shape, dt, kind="ExternalOutput")
        t = self.dbg_outs[name]

        def f(e):
            e.dma_start(t.ap()[:, :], ap_fn()).then_inc(self.sO.h, 16)
        self.sO.n += 16
        blk.gpsimd(f)

    def p0_setup(self):
        nc = self.nc
        with nc.Block() as blk:
            def loads(e):
                e.dma_start(self.pvec[:, :], self.pvec_d.ap()[:, :]).then_inc(self.sD.h, 16)
                e.dma_start(self.brow[:, :], self.brow_d.ap()[:, :]).then_inc(self.sD.h, 16)
                e.dma_start(self.qsks[:, :], self.qsks_d.ap()[:, :]).then_inc(self.sD.h, 16)
                e.dma_start(self.ident[:, :], self.ident_d.ap()[:, :]).then_inc(self.sD.h, 16)
                e.dma_start(self.masks[:, :].rearrange("p (v n) -> p v n", v=3),
                            self.lmask_d.ap()[:, :, :].rearrange("v p n -> p v n")
                            ).then_inc(self.sD.h, 16)
                self.sD.n += 5 * 16
                e.wait_ge(self.sD.h, self.sD.n)
            blk.sync(loads)

            def vset(e):
                e.wait_ge(self.sD.h, self.sD.n)
                e.memset(self.ones_k[:, :], 1.0)
                e.memset(self.ones_m[:, :], 1.0)
                e.memset(self.ones_mf[:, :], 1.0)
                e.memset(self.eps24[:, :], 1e-24)
                e.memset(self.eps5[:, :], 1e-5)
                e.tensor_copy(self.brow_bf[:, :], self.brow[:, :])
            blk.vector(vset)

        with nc.Block() as blk, nc.psum_tensor("p0ps", [128, 16], F32) as pps:
            def v1(e):
                self.sV.inc(e.reduce_max(self.tiny[0:NL, 0:2],
                                         self.qsks[:, :].rearrange("l (h d) -> l h d", h=2),
                                         AX.X, apply_absolute_value=True))
            blk.vector(v1)

            def dmamove(e):
                e.wait_ge(self.sV.h, self.sV.n)
                e.dma_start(self.stat_sc.ap()[0:NL, :], self.tiny[0:NL, 0:2]
                            ).then_inc(self.sD.h, 16)
                self.sD.n += 16
                e.wait_ge(self.sD.h, self.sD.n)
                e.dma_start(self.tiny[0:1, 8:8 + 2 * NL],
                            self.stat_sc.ap()[0:NL, :].rearrange("l h -> (l h)"
                                                                 ).unsqueeze(0)
                            ).then_inc(self.sD.h, 16)
                self.sD.n += 16
            blk.sync(dmamove)

            def pe1(e):
                e.wait_ge(self.sD.h, self.sD.n)
                self.sP.inc(e.matmul(pps[:, 0:2 * NL], self.ones_mf[:, :],
                                     self.tiny[0:1, 8:8 + 2 * NL],
                                     start=True, stop=True))
            blk.tensor(pe1)

            def a2(e):
                e.wait_ge(self.sP.h, self.sP.n)
                self.sA.inc(e.activation(self.tiny[:, 8:8 + 2 * NL], pps[:, 0:2 * NL],
                                         AF.Copy))
            blk.scalar(a2)

            def v2(e):
                e.wait_ge(self.sA.h, self.sA.n)
                pairs = self.tiny[:, 8:8 + 2 * NL].rearrange("p (l h) -> p l h", l=NL)
                e.tensor_tensor(self.negc128[:, :], pairs[:, :, 0:1].squeeze(2),
                                pairs[:, :, 1:2].squeeze(2), ALU.mult)
                e.drain()
                self.sV.inc(e.tensor_scalar(self.negc128[:, :], self.negc128[:, :],
                                            -8.0, None, ALU.mult))
            blk.vector(v2)

    def p1_cast_weights(self):
        nc = self.nc
        jobs = []
        jobs.append((self.wqT.ap(), self.wqT_b.ap(), D, D))
        jobs.append((self.wkT.ap(), self.wkT_b.ap(), D, D))
        jobs.append((self.wvT.ap(), self.wvT_b.ap(), D, D))
        jobs.append((self.woT.ap(), self.woT_b.ap(), D, D))
        for l in range(NL):
            jobs.append((self.wqkv.ap()[l], self.wqkv_b.ap()[l], D, 3 * D))
            jobs.append((self.wom.ap()[l], self.wom_b.ap()[l], D, D))
            jobs.append((self.w1p.ap()[l], self.w1_b.ap()[l], D, DI2))
            jobs.append((self.w2p.ap()[l], self.w2_b.ap()[l], DIP, D))
        CMAX = 24576
        jobs2 = []
        for (s_, d_, rows, cols) in jobs:
            rc = rows // 128
            if rc * cols <= CMAX:
                jobs2.append((s_, d_, rows, cols))
            else:
                half = rows // 2
                jobs2.append((s_[0:half, :], d_[0:half, :], half, cols))
                jobs2.append((s_[half:rows, :], d_[half:rows, :], half, cols))
        with nc.sbuf_tensor("castf", [128, CMAX], F32) as cf, \
             nc.sbuf_tensor("castb", [128, CMAX], BF16) as cb:
            for (src, dst, rows, cols) in jobs2:
                rc = rows // 128
                assert rc * cols <= CMAX
                with nc.Block() as blk:
                    def ld(e, src=src, rc=rc, cols=cols):
                        e.dma_start(sch(cf[:, 0:rc * cols], rc),
                                    dch(src)).then_inc(self.sD.h, 16)
                        self.sD.n += 16
                    blk.sync(ld)

                    def cast(e, rc=rc, cols=cols):
                        e.wait_ge(self.sD.h, self.sD.n)
                        e.wait_ge(self.sO.h, self.sO.n)
                        self.sV.inc(e.tensor_copy(cb[:, 0:rc * cols], cf[:, 0:rc * cols]))
                    blk.vector(cast)

                    def st(e, dst=dst, rc=rc, cols=cols):
                        e.wait_ge(self.sV.h, self.sV.n)
                        e.dma_start(dch(dst),
                                    sch(cb[:, 0:rc * cols], rc)).then_inc(self.sO.h, 16)
                        self.sO.n += 16
                    blk.sync(st)

    def p2_ln(self, upto=99, nwin=None):
        nc = self.nc
        with nc.sbuf_tensor("xw", [128, NCH * LW], F32) as xw, \
             nc.sbuf_tensor("xbf", [128, NCH * LW], BF16) as xbf, \
             nc.sbuf_tensor("x2bf", [128, NCH * LW], BF16) as x2bf, \
             nc.sbuf_tensor("xnf", [128, NCH * LW], F32) as xnf, \
             nc.sbuf_tensor("xnb", [128, NCH * LW], BF16) as xnb, \
             nc.sbuf_tensor("st2", [1, 4 * LW], F32) as st2, \
             nc.psum_tensor("lps", [1, 2 * LW], F32) as lps, \
             nc.psum_tensor("lpb", [128, 2 * LW], F32) as lpb:
            for w in range(nwin or (S // LW)):
                w0 = w * LW
                with nc.Block() as blk:
                    def ld(e, w0=w0):
                        e.dma_start(sch(xw[:, :], NCH),
                                    dch(self.xT.ap()[:, w0:w0 + LW])
                                    ).then_inc(self.sD.h, 16)
                        self.sD.n += 16
                        e.wait_ge(self.sD.h, self.sD.n)
                    blk.sync(ld)

                    def cast(e):
                        e.wait_ge(self.sD.h, self.sD.n)
                        self.sV.inc(e.tensor_copy(xbf[:, :], xw[:, :]))
                    if upto > 0: blk.vector(cast)

                    def sq(e):
                        e.wait_ge(self.sV.h, self.sV.n)
                        self.sA.inc(e.activation(x2bf[:, :], xbf[:, :], AF.Square))
                    if upto > 1: blk.scalar(sq)

                    def stats(e):
                        e.wait_ge(self.sA.h, self.sA.n)
                        for c in range(NCH):
                            e.matmul(lps[0:1, 0:LW], self.ones_k[:, :],
                                     xbf[:, c * LW:(c + 1) * LW],
                                     start=(c == 0), stop=(c == NCH - 1))
                        for c in range(NCH):
                            ins_ = e.matmul(lps[0:1, LW:2 * LW], self.ones_k[:, :],
                                            x2bf[:, c * LW:(c + 1) * LW],
                                            start=(c == 0), stop=(c == NCH - 1))
                        self.sP.inc(ins_)
                    if upto > 2: blk.tensor(stats)

                    def astats(e):
                        e.wait_ge(self.sP.h, self.sP.n)
                        e.activation(st2[0:1, 0:LW], lps[0:1, 0:LW], AF.Copy, scale=1.0 / D)
                        self.sA.inc(e.activation(st2[0:1, LW:2 * LW], lps[0:1, LW:2 * LW],
                                                  AF.Copy, scale=1.0 / D))
                    if upto > 3: blk.scalar(astats)

                    def vstats(e):
                        e.wait_ge(self.sA.h, self.sA.n)
                        e.tensor_tensor(st2[0:1, 2 * LW:3 * LW], st2[0:1, 0:LW],
                                        st2[0:1, 0:LW], ALU.mult)
                        e.drain()
                        self.sV.inc(e.tensor_tensor(st2[0:1, LW:2 * LW], st2[0:1, LW:2 * LW],
                                                     st2[0:1, 2 * LW:3 * LW], ALU.subtract))
                    if upto > 3: blk.vector(vstats)

                    def rstd(e):
                        e.wait_ge(self.sV.h, self.sV.n)
                        e.activation(st2[0:1, 2 * LW:3 * LW], st2[0:1, LW:2 * LW],
                                     AF.Ln, bias=self.eps5[0:1, :])
                        e.drain()
                        self.sA.inc(e.activation(st2[0:1, 2 * LW:3 * LW],
                                                  st2[0:1, 2 * LW:3 * LW], AF.Exp, scale=-0.5))
                    if upto > 4: blk.scalar(rstd)

                    def vb(e):
                        e.wait_ge(self.sA.h, self.sA.n)
                        self.sV.inc(e.scalar_tensor_tensor(st2[0:1, 3 * LW:4 * LW],
                                                            st2[0:1, 0:LW], -1.0,
                                                            st2[0:1, 2 * LW:3 * LW],
                                                            ALU.mult, ALU.mult))
                    if upto > 5: blk.vector(vb)

                    def bcast(e):
                        e.wait_ge(self.sV.h, self.sV.n)
                        e.matmul(lpb[:, 0:LW], self.ones_mf[:, :],
                                 st2[0:1, 2 * LW:3 * LW], start=True, stop=True)
                        self.sP.inc(e.matmul(lpb[:, LW:2 * LW], self.ones_mf[:, :],
                                              st2[0:1, 3 * LW:4 * LW], start=True, stop=True))
                    if upto > 6: blk.tensor(bcast)

                    def vxn(e):
                        e.wait_ge(self.sP.h, self.sP.n)
                        for c in range(NCH):
                            cs = slice(c * LW, (c + 1) * LW)
                            e.tensor_tensor(xnf[:, cs], lpb[:, 0:LW], xw[:, cs], ALU.mult)
                            e.drain()
                            e.tensor_tensor(xnf[:, cs], lpb[:, LW:2 * LW], xnf[:, cs], ALU.add)
                        e.drain()
                        self.sV.inc(e.tensor_copy(xnb[:, :], xnf[:, :]))
                    if upto > 7: blk.vector(vxn)

                    def st(e, w0=w0):
                        e.wait_ge(self.sV.h, self.sV.n)
                        e.dma_start(dch(self.xn_b.ap()[:, w0:w0 + LW]),
                                    sch(xnb[:, :], NCH)).then_inc(self.sO.h, 16)
                        e.dma_start(dch(self.xmA.ap()[:, w0:w0 + LW]),
                                    sch(xnf[:, :], NCH)).then_inc(self.sO.h, 16)
                        self.sO.n += 32
                    if upto > 8: blk.sync(st)

    def p34_outer(self):
        nc = self.nc
        with nc.sbuf_tensor("kT_res", [128, NCH * S], BF16) as kT:
            self._p3_kv(kT)
            self._p4_att(kT)

    def _p3_kv(self, kT):
        nc = self.nc
        with nc.sbuf_tensor("wk_sb", [128, NCH * D], BF16) as wk, \
             nc.sbuf_tensor("wv_sb", [128, NCH * D], BF16) as wv, \
             nc.sbuf_tensor("xnw3", [128, NCH * LW], BF16) as xnw, \
             nc.sbuf_tensor("vw3", [128, D], BF16) as vw, \
             nc.psum_tensor("pk3", [128, 2 * 512], F32) as pk, \
             nc.psum_tensor("pv3", [128, D], F32) as pv:
            with nc.Block() as blk:
                def ld(e):
                    e.wait_ge(self.sO.h, self.sO.n)
                    e.dma_start(sch(wk[:, :], NCH), dch(self.wkT_b.ap())).then_inc(self.sD.h, 16)
                    e.dma_start(sch(wv[:, :], NCH), dch(self.wvT_b.ap())).then_inc(self.sD.h, 16)
                    self.sD.n += 32
                blk.sync(ld)
            for w in range(S // LW):
                w0 = w * LW
                with nc.Block() as blk:
                    def ld(e, w0=w0):
                        e.dma_start(sch(xnw[:, :], NCH),
                                    dch(self.xn_b.ap()[:, w0:w0 + LW])).then_inc(self.sD.h, 16)
                        self.sD.n += 16
                    blk.sync(ld)

                    evac_cnt = [self.sA.n]
                    for ec in range(NCH):
                        bank = ec % 2

                        def pe(e, ec=ec, bank=bank):
                            if ec == 0:
                                e.wait_ge(self.sD.h, self.sD.n)
                            if ec >= 2:
                                e.wait_ge(self.sA.h, evac_cnt[0] + ec - 1)
                            for dc in range(NCH):
                                i_ = e.matmul(pk[:, bank * 512:bank * 512 + 512],
                                              wk[:, dc * D + ec * 128:dc * D + (ec + 1) * 128],
                                              xnw[:, dc * LW:(dc + 1) * LW],
                                              start=(dc == 0), stop=(dc == NCH - 1))
                            self.sP.inc(i_)
                        blk.tensor(pe)

                        def ac(e, ec=ec, bank=bank, w0=w0):
                            e.wait_ge(self.sP.h, self.sP.n)
                            self.sA.inc(e.activation(
                                kT[:, ec * S + w0:ec * S + w0 + LW],
                                pk[:, bank * 512:bank * 512 + 512], AF.Identity,
                                bias=self.pvec[:, PV_BK + ec:PV_BK + ec + 1]))
                        blk.scalar(ac)

                    for t in range(LW // 128):
                        def pe(e, t=t):
                            e.wait_ge(self.sA.h, self.sA.n)
                            for half in range(2):
                                hs = slice(half * 512, half * 512 + 512)
                                for dc in range(NCH):
                                    e.matmul(pv[:, hs],
                                             xnw[:, dc * LW + t * 128:dc * LW + (t + 1) * 128],
                                             wv[:, dc * D + half * 512:dc * D + half * 512 + 512],
                                             start=(dc == 0), stop=False)
                                i_ = e.matmul(pv[:, hs], self.ones_m[:, :],
                                              self.brow_bf[0:1, hs], start=False, stop=True)
                            self.sP.inc(i_)
                        blk.tensor(pe)

                        def ac(e):
                            e.wait_ge(self.sP.h, self.sP.n)
                            e.wait_ge(self.sO.h, self.sO.n)
                            self.sA.inc(e.activation(vw[:, :], pv[:, :], AF.Copy))
                        blk.scalar(ac)

                        def st(e, w0=w0, t=t):
                            e.wait_ge(self.sA.h, self.sA.n)
                            e.dma_start(self.vsc.ap()[w0 + t * 128:w0 + (t + 1) * 128, :],
                                        vw[:, :]).then_inc(self.sO.h, 16)
                            self.sO.n += 16
                        blk.sync(st)

    def _p4_att(self, kT):
        nc = self.nc
        NB = S // LW
        from contextlib import ExitStack
        es4 = ExitStack()
        wq = es4.enter_context(nc.sbuf_tensor("wq_sb", [128, NCH * D], BF16))
        wo = es4.enter_context(nc.sbuf_tensor("wo_sb", [128, NCH * D], BF16))
        xnq = es4.enter_context(nc.sbuf_tensor("xnq", [128, NCH * LW], BF16))
        qT = es4.enter_context(nc.sbuf_tensor("qT4", [128, NCH * LW], BF16))
        theta = es4.enter_context(nc.sbuf_tensor("theta", [128, S], F32))
        thA = es4.enter_context(nc.sbuf_tensor("thA", [128, 512], F32))
        thB = es4.enter_context(nc.sbuf_tensor("thB", [128, 512], F32))
        psb = es4.enter_context(nc.sbuf_tensor("p4p", [128, S], BF16))
        pT4 = es4.enter_context(nc.sbuf_tensor("pT4", [128, 4 * S], BF16))
        att4 = es4.enter_context(nc.sbuf_tensor("att4", [128, 4 * D], BF16))
        aoT = es4.enter_context(nc.sbuf_tensor("aoT4", [128, NCH * LW], BF16))
        vch = es4.enter_context(nc.sbuf_tensor("vch", [128, 2 * D], BF16))
        rmx = es4.enter_context(nc.sbuf_tensor("rmx", [128, 1], F32))
        rsum = es4.enter_context(nc.sbuf_tensor("rsum", [128, 1], F32))
        rcp = es4.enter_context(nc.sbuf_tensor("rcp", [128, 4], F32))
        scp = es4.enter_context(nc.psum_tensor("sc4", [128, 2 * 512], F32))
        trp = es4.enter_context(nc.psum_tensor("tr4", [128, 1024], BF16))
        avp = es4.enter_context(nc.psum_tensor("av4", [128, 2 * D], F32))
        if True:
            with nc.Block() as blk:
                def ld(e):
                    e.wait_ge(self.sO.h, self.sO.n)
                    e.dma_start(sch(wq[:, :], NCH), dch(self.wqT_b.ap())).then_inc(self.sD.h, 16)
                    e.dma_start(sch(wo[:, :], NCH), dch(self.woT_b.ap())).then_inc(self.sD.h, 16)
                    self.sD.n += 32
                blk.sync(ld)
            for b in range(NB):
                b0 = b * LW
                with nc.Block() as blk:
                    def ld(e, b0=b0):
                        e.dma_start(sch(xnq[:, :], NCH),
                                    dch(self.xn_b.ap()[:, b0:b0 + LW])).then_inc(self.sD.h, 16)
                        self.sD.n += 16
                    blk.sync(ld)

                    for ec in range(NCH):
                        bank = ec % 2
                        acn = [0]

                        def pe(e, ec=ec, bank=bank):
                            if ec == 0:
                                e.wait_ge(self.sD.h, self.sD.n)
                                e.wait_ge(self.sO.h, self.sO.n)
                            if ec >= 2:
                                e.wait_ge(self.sA.h, self.sA.n - 1)
                            for dc in range(NCH):
                                i_ = e.matmul(scp[:, bank * 512:bank * 512 + 512],
                                              wq[:, dc * D + ec * 128:dc * D + (ec + 1) * 128],
                                              xnq[:, dc * LW:(dc + 1) * LW],
                                              start=(dc == 0), stop=(dc == NCH - 1))
                            self.sP.inc(i_)
                        blk.tensor(pe)

                        def ac(e, ec=ec, bank=bank):
                            e.wait_ge(self.sP.h, self.sP.n)
                            self.sA.inc(e.activation(
                                qT[:, ec * LW:(ec + 1) * LW],
                                scp[:, bank * 512:bank * 512 + 512], AF.Identity,
                                bias=self.pvec[:, PV_BQ + ec:PV_BQ + ec + 1]))
                        blk.scalar(ac)

                for t in range(4):
                  with nc.Block() as blk:
                    if True:
                        for kc in range(NCH):
                            bank = kc % 2

                            def pe(e, t=t, kc=kc, bank=bank):
                                e.wait_ge(self.sA.h, self.sA.n)
                                if kc >= 2:
                                    e.wait_ge(self.sV.h, self.sV.n - 3 * 1)
                                for dc in range(NCH):
                                    i_ = e.matmul(scp[:, bank * 512:bank * 512 + 512],
                                                  qT[:, dc * LW + t * 128:dc * LW + (t + 1) * 128],
                                                  kT[:, dc * S + kc * 512:dc * S + (kc + 1) * 512],
                                                  start=(dc == 0), stop=(dc == NCH - 1))
                                self.sP.inc(i_)
                            blk.tensor(pe)

                            def ac(e, kc=kc, bank=bank):
                                e.wait_ge(self.sP.h, self.sP.n)
                                e.wait_ge(self.sV.h, self.sV.n)
                                e.activation(thA[:, :], scp[:, bank * 512:bank * 512 + 512],
                                             AF.Tanh, scale=0.5)
                                self.sA.inc(e.activation(
                                    thB[:, :], scp[:, bank * 512:bank * 512 + 512], AF.Tanh))
                            blk.scalar(ac)

                            def ve(e, kc=kc, bank=bank):
                                e.wait_ge(self.sA.h, self.sA.n)
                                cs = slice(kc * 512, (kc + 1) * 512)
                                e.tensor_scalar(theta[:, cs], scp[:, bank * 512:bank * 512 + 512],
                                                0.0, 0.1, ALU.max, ALU.mult)
                                e.drain()
                                e.scalar_tensor_tensor(theta[:, cs], thA[:, :], 0.1,
                                                       theta[:, cs], ALU.mult, ALU.add)
                                e.drain()
                                self.sV.inc(e.scalar_tensor_tensor(
                                    theta[:, cs], thB[:, :], 0.15, theta[:, cs],
                                    ALU.mult, ALU.add), 3)
                            blk.vector(ve)

                        def vm(e):
                            e.drain()
                            self.sV.inc(e.reduce_max(rmx[:, :], theta[:, :], AX.X, negate=True))
                        blk.vector(vm)

                        def ax(e):
                            e.wait_ge(self.sV.h, self.sV.n)
                            self.sA.inc(e.activation(psb[:, :], theta[:, :], AF.Exp,
                                                     bias=rmx[:, :]))
                        blk.scalar(ax)

                        def vr(e, t=t, b=b):
                            e.wait_ge(self.sA.h, self.sA.n)
                            e.reduce_sum(rsum[:, :], psb[:, :], AX.X)
                            e.drain()
                            e.reciprocal(rcp[:, t:t + 1], rsum[:, :])
                            if b == 0 and "p4s" in self.debug:
                                e.tensor_copy(self.snap4[:, t * 80:t * 80 + 32], theta[:, 0:32])
                                e.tensor_copy(self.snap4[:, t * 80 + 32:t * 80 + 64], psb[:, 0:32])
                                e.tensor_copy(self.snap4[:, t * 80 + 64:t * 80 + 65], rsum[:, :])
                                e.tensor_copy(self.snap4[:, t * 80 + 65:t * 80 + 66], rmx[:, :])
                                e.drain()
                                e.tensor_copy(self.snap4[:, t * 80 + 66:t * 80 + 67], rcp[:, t:t + 1])
                            self.sV.inc(e.drain())
                        blk.vector(vr)

                        for grp in range(4):
                            def pe(e, t=t, grp=grp):
                                if grp == 0:
                                    e.wait_ge(self.sA.h, self.sA.n)
                                else:
                                    e.wait_ge(self.sV.h, self.sV.n)
                                for j8 in range(8):
                                    j = grp * 8 + j8
                                    i_ = e.matmul(trp[:, j8 * 128:(j8 + 1) * 128],
                                                  psb[:, j * 128:(j + 1) * 128],
                                                  self.ident[:, :], is_transpose=True,
                                                  start=(j8 == 0), stop=(j8 == 7))
                                self.sP.inc(i_)
                            blk.tensor(pe)

                            def vt(e, t=t, grp=grp):
                                e.wait_ge(self.sP.h, self.sP.n)
                                self.sV.inc(e.tensor_copy(
                                    pT4[:, t * S + grp * 1024:t * S + (grp + 1) * 1024],
                                    trp[:, :]))
                            blk.vector(vt)

                for hv in range(2):
                  with nc.Block() as blk:
                    if True:
                        tl0, tl1 = 2 * hv, 2 * hv + 1
                        for tc in range(S // 128):
                            def ldv(e, tc=tc, hv=hv):
                                if tc >= 2 or hv > 0:
                                    e.wait_ge(self.sP.h, self.sP.n)
                                e.dma_start(vch[:, (tc % 2) * D:(tc % 2 + 1) * D],
                                            self.vsc.ap()[tc * 128:(tc + 1) * 128, :]
                                            ).then_inc(self.sD.h, 16)
                                self.sD.n += 16
                            blk.sync(ldv)

                            def pe(e, tc=tc, tl0=tl0, tl1=tl1):
                                e.wait_ge(self.sD.h, self.sD.n)
                                if tc == 0:
                                    e.wait_ge(self.sV.h, self.sV.n)
                                for ti, tl in ((0, tl0), (1, tl1)):
                                    for half in range(2):
                                        i_ = e.matmul(
                                            avp[:, ti * D + half * 512:ti * D + half * 512 + 512],
                                            pT4[:, tl * S + tc * 128:tl * S + (tc + 1) * 128],
                                            vch[:, (tc % 2) * D + half * 512:(tc % 2) * D + half * 512 + 512],
                                            start=(tc == 0), stop=(tc == S // 128 - 1))
                                self.sP.inc(i_)
                            blk.tensor(pe)

                        def vav(e, hv=hv, tl0=tl0, tl1=tl1):
                            e.wait_ge(self.sP.h, self.sP.n)
                            for ti, tl in ((0, tl0), (1, tl1)):
                                i_ = e.tensor_scalar(att4[:, tl * D:(tl + 1) * D],
                                                     avp[:, ti * D:(ti + 1) * D],
                                                     rcp[:, tl:tl + 1], None, ALU.mult)
                            self.sV.inc(i_)
                        blk.vector(vav)

                with nc.Block() as blk:
                    for tl in range(4):
                        def pe(e, tl=tl):
                            e.wait_ge(self.sV.h, self.sV.n)
                            for j in range(8):
                                i_ = e.matmul(trp[:, j * 128:(j + 1) * 128],
                                              att4[:, tl * D + j * 128:tl * D + (j + 1) * 128],
                                              self.ident[:, :], is_transpose=True,
                                              start=(j == 0), stop=(j == 7))
                            self.sP.inc(i_)
                        blk.tensor(pe)

                        def vt(e, tl=tl):
                            e.wait_ge(self.sP.h, self.sP.n)
                            self.sV.inc(e.tensor_copy(
                                sch(aoT[:, :], NCH)[:, :, tl * 128:(tl + 1) * 128],
                                sch(trp[:, :], 8)))
                        blk.vector(vt)

                    for ec in range(NCH):
                        bank = ec % 2

                        def pe(e, ec=ec, bank=bank):
                            e.wait_ge(self.sV.h, self.sV.n)
                            if ec >= 2:
                                e.wait_ge(self.sA.h, self.sA.n - 1)
                            for dc in range(NCH):
                                i_ = e.matmul(scp[:, bank * 512:bank * 512 + 512],
                                              wo[:, dc * D + ec * 128:dc * D + (ec + 1) * 128],
                                              aoT[:, dc * LW:(dc + 1) * LW],
                                              start=(dc == 0), stop=(dc == NCH - 1))
                            self.sP.inc(i_)
                        blk.tensor(pe)

                        def ac(e, ec=ec, bank=bank):
                            e.wait_ge(self.sP.h, self.sP.n)
                            if ec < 2:
                                e.wait_ge(self.sO.h, self.sO.n)
                            self.sA.inc(e.activation(
                                theta[:, ec * LW:(ec + 1) * LW],
                                scp[:, bank * 512:bank * 512 + 512], AF.Identity,
                                bias=self.pvec[:, PV_BO + ec:PV_BO + ec + 1]))
                        blk.scalar(ac)

                    def st(e, b0=b0):
                        e.wait_ge(self.sA.h, self.sA.n)
                        e.dma_start(dch(self.ssc.ap()[:, b0:b0 + LW]),
                                    sch(theta[:, :], NCH)).then_inc(self.sO.h, 16)
                        self.sO.n += 16
                    blk.sync(st)
                    if b == 0 and "p4s" in self.debug:
                        self.tap5(blk, "snap4", self.snap4, [128, 320], F32)
                    if b == 0 and "p4d" in self.debug:
                        self.tap5(blk, "qT4", qT, [128, NCH * LW], BF16)
                        self.tap5(blk, "pT4", pT4, [128, 4 * S], BF16)
                        self.tap5(blk, "att4", att4, [128, 4 * D], BF16)
                        self.tap5(blk, "aoT4", aoT, [128, NCH * LW], BF16)
                        self.tap5(blk, "psb4", psb, [128, S], BF16)
                        self.tap5(blk, "rcp4", rcp, [128, 4], F32)
        es4.close()

    def p5_mini(self, layers=None):
        for l in range(layers if layers is not None else NL):
            self._p5a_attn(l)
            self._p5b_ffn(l)

    def _p5a_attn(self, l):
        nc = self.nc
        from contextlib import ExitStack
        es = ExitStack()
        sb = lambda n_, sh_, dt_: es.enter_context(nc.sbuf_tensor(n_ + f"_{l}a", sh_, dt_))
        wqkv = sb("wqkv5", [128, NCH * 3 * D], BF16)
        wom = sb("wom5", [128, NCH * D], BF16)
        x_win = sb("xwin5", [128, NCH * 513], F32)
        x_sh = sb("xsh5", [128, NCH * WA], BF16)
        qtok = sb("qtok5", [128, 4 * D], BF16)
        ktok = sb("ktok5", [128, D], BF16)
        v_roll = sb("vroll5", [128, 6 * D], BF16)
        kT_roll = sb("ktr5", [128, NCH * 768], BF16)
        qTw = sb("qtw5", [128, NCH * WA], BF16)
        pT_h = sb("pth5", [128, 6 * WA], BF16)
        aoT = sb("aot5", [128, NCH * WA], BF16)
        tabs = sb("tabs5", [128, 4 * 4 * DH], F32)
        teff = sb("teff5", [128, 4 * 4 * DH], F32)
        qsb4 = sb("qsb5", [128, 4 * DH], F32)
        rotA = sb("rota5", [128, D], F32)
        rotB = sb("rotb5", [128, D], F32)
        sqb = sb("sqb5", [128, D], F32)
        stats = sb("stats5", [128, 4 * HEADS], F32)
        kst = sb("kst5", [128, HEADS], F32)
        sst = sb("sst5", [128, 2 * HEADS], F32)
        p5p = sb("p5p", [128, 384], BF16)
        tmsk = sb("tmsk5", [128, 384], F32)
        rs5 = sb("rs5", [128, 1], F32)
        rc5 = sb("rc5", [128, 1], F32)
        qrow = sb("qrow5", [1, 2 * DH], F32)
        dbs = sb("dbs5", [128, 1536], F32)
        pp = es.enter_context(nc.psum_tensor(f"pp5_{l}a", [128, 3 * D], F32))
        ptr = es.enter_context(nc.psum_tensor(f"ptr5_{l}a", [128, 1024], BF16))
        pbc = es.enter_context(nc.psum_tensor(f"pbc5_{l}a", [128, 512], F32))

        TQ1, TQ2, TK1, TK2 = 0, 1, 2, 3

        def tab(buf, i, t=None):
            base = i * 4 * DH
            if t is None:
                return sch(buf[:, base:base + 4 * DH], 4)
            return buf[:, base + t * DH:base + (t + 1) * DH]

        with nc.Block() as blk:
            def ld(e):
                e.wait_ge(self.sO.h, self.sO.n)
                e.dma_start(sch(wqkv[:, :], NCH), dch(self.wqkv_b.ap()[l])).then_inc(self.sD.h, 16)
                e.dma_start(sch(wom[:, :], NCH), dch(self.wom_b.ap()[l])).then_inc(self.sD.h, 16)
                e.dma_start(qrow[:, :], self.qsks_d.ap()[l:l + 1, :]).then_inc(self.sD.h, 16)
                self.sD.n += 48
            blk.sync(ld)

            def vz(e):
                e.wait_ge(self.sD.h, self.sD.n)
                e.memset(kT_roll[:, :], 0.0)
                e.memset(v_roll[:, :], 0.0)
                self.sV.inc(e.memset(pT_h[:, :], 0.0))
            blk.vector(vz)

            def pe(e):
                e.wait_ge(self.sD.h, self.sD.n)
                e.matmul(pbc[:, 0:DH], self.ones_mf[:, :], qrow[0:1, 0:DH],
                         start=True, stop=True)
                self.sP.inc(e.matmul(pbc[:, DH:2 * DH], self.ones_mf[:, :],
                                     qrow[0:1, DH:2 * DH], start=True, stop=True))
            blk.tensor(pe)

            def ac(e):
                e.wait_ge(self.sP.h, self.sP.n)
                e.activation(qsb4[:, 0:DH], pbc[:, 0:DH], AF.Copy)
                self.sA.inc(e.activation(qsb4[:, 2 * DH:3 * DH], pbc[:, DH:2 * DH], AF.Copy))
            blk.scalar(ac)

            def vr(e):
                e.wait_ge(self.sA.h, self.sA.n)
                e.tensor_copy(qsb4[:, DH:DH + 32], qsb4[:, 32:64])
                e.tensor_copy(qsb4[:, DH + 32:2 * DH], qsb4[:, 0:32])
                e.tensor_copy(qsb4[:, 3 * DH:3 * DH + 32], qsb4[:, 2 * DH + 32:3 * DH])
                self.sV.inc(e.tensor_copy(qsb4[:, 3 * DH + 32:4 * DH],
                                          qsb4[:, 2 * DH:2 * DH + 32]))
            blk.vector(vr)

        for w in range(S // WA):
            w0 = w * WA
            with nc.Block() as blk:
                def ld(e, w=w, w0=w0):
                    if w == 0:
                        e.dma_start(sch(x_win[:, :], NCH)[:, :, 1:513],
                                    dch(self.xmA.ap()[:, 0:WA])).then_inc(self.sD.h, 16)
                    else:
                        e.dma_start(sch(x_win[:, :], NCH),
                                    dch(self.xmA.ap()[:, w0 - 1:w0 + WA])).then_inc(self.sD.h, 16)
                    for i, td in enumerate((self.t1q_d, self.t2q_d, self.t1k_d, self.t2k_d)):
                        e.dma_start(sch(tabs[:, :], 16)[:, 4 * i:4 * i + 4, :],
                                    td.ap()[w0:w0 + WA, :].rearrange("(t p) d -> p t d", p=128)
                                    ).then_inc(self.sD.h, 16)
                    self.sD.n += 16 * 5
                blk.sync(ld)

                def vprep(e, w=w):
                    e.wait_ge(self.sD.h, self.sD.n)
                    if w == 0:
                        e.memset(sch(x_win[:, :], NCH)[:, :, 0:1], 0.0)
                    e.tensor_copy(sch(kT_roll[:, :], NCH)[:, :, 0:256],
                                  sch(kT_roll[:, :], NCH)[:, :, 512:768])
                    e.tensor_copy(sch(x_sh[:, :], NCH)[:, 0:4, :],
                                  sch(x_win[:, :], NCH)[:, 0:4, 1:513])
                    e.tensor_copy(sch(x_sh[:, :], NCH)[:, 4:NCH, :],
                                  sch(x_win[:, :], NCH)[:, 4:NCH, 0:512])
                    for i, sc_ in ((TQ1, 0), (TQ2, 1), (TK1, 2), (TK2, 3)):
                        i_ = e.tensor_tensor(
                            tab(teff, i), tab(tabs, i),
                            qsb4[:, sc_ * DH:(sc_ + 1) * DH].unsqueeze(1).broadcast_to([128, 4, DH]),
                            ALU.mult)
                    self.sV.inc(i_)
                blk.vector(vprep)

                if w == 0 and "m5" in self.debug:
                    self.tap5(blk, "negc", self.negc128, [128, NL], F32)
                    self.tap5(blk, "xsh", x_sh, [128, NCH * WA], BF16)
                    self.tap5(blk, "teff", teff, [128, 16 * DH], F32)
                    self.tap5(blk, "qsb4", qsb4, [128, 4 * DH], F32)

                for t in range(4):
                    g = w * 4 + t
                    slot = g % 6

                    def pe(e, t=t):
                        e.wait_ge(self.sV.h, self.sV.n)
                        if t > 0:
                            e.wait_ge(self.sA.h, self.sA.n)
                        for dc in range(NCH):
                            for ee in range(6):
                                i_ = e.matmul(pp[:, ee * 512:(ee + 1) * 512],
                                              x_sh[:, dc * WA + t * 128:dc * WA + (t + 1) * 128],
                                              wqkv[:, dc * 3 * D + ee * 512:dc * 3 * D + (ee + 1) * 512],
                                              start=(dc == 0), stop=(dc == NCH - 1))
                        self.sP.inc(i_)
                    blk.tensor(pe)

                    def asq(e):
                        e.wait_ge(self.sP.h, self.sP.n)
                        self.sA.inc(e.activation(sqb[:, :], pp[:, 0:D], AF.Square))
                    blk.scalar(asq)

                    def vss(e):
                        e.wait_ge(self.sA.h, self.sA.n)
                        self.sV.inc(e.reduce_sum(sst[:, 0:HEADS], sch(sqb[:, :], HEADS), AX.X))
                    blk.vector(vss)

                    def asq2(e):
                        e.wait_ge(self.sV.h, self.sV.n)
                        self.sA.inc(e.activation(sqb[:, :], pp[:, D:2 * D], AF.Square))
                    blk.scalar(asq2)

                    def vss2(e):
                        e.wait_ge(self.sA.h, self.sA.n)
                        self.sV.inc(e.reduce_sum(sst[:, HEADS:2 * HEADS],
                                                 sch(sqb[:, :], HEADS), AX.X))
                    blk.vector(vss2)

                    def arst(e, t=t):
                        e.wait_ge(self.sV.h, self.sV.n)
                        e.activation(sst[:, 0:HEADS], sst[:, 0:HEADS], AF.Ln,
                                     bias=self.eps24[:, :])
                        e.drain()
                        e.activation(stats[:, t * HEADS:(t + 1) * HEADS], sst[:, 0:HEADS],
                                     AF.Exp, scale=-0.5)
                        e.activation(sst[:, HEADS:2 * HEADS], sst[:, HEADS:2 * HEADS],
                                     AF.Ln, bias=self.eps24[:, :])
                        e.drain()
                        self.sA.inc(e.activation(kst[:, :], sst[:, HEADS:2 * HEADS],
                                                 AF.Exp, scale=-0.5))
                    blk.scalar(arst)

                    def vsc8(e, t=t):
                        e.wait_ge(self.sA.h, self.sA.n)
                        self.sV.inc(e.tensor_scalar(stats[:, t * HEADS:(t + 1) * HEADS],
                                                    stats[:, t * HEADS:(t + 1) * HEADS],
                                                    8.0, None, ALU.mult))
                    blk.vector(vsc8)

                    def vrotq(e, t=t):
                        qv = sch(pp[:, 0:D], HEADS)
                        e.tensor_tensor(sch(rotA[:, :], HEADS), qv,
                                        tab(teff, TQ1, t).unsqueeze(1).broadcast_to([128, HEADS, DH]),
                                        ALU.mult)
                        e.tensor_tensor(sch(rotB[:, :], HEADS)[:, :, 0:32], qv[:, :, 32:64],
                                        tab(teff, TQ2, t)[:, 0:32].unsqueeze(1).broadcast_to([128, HEADS, 32]),
                                        ALU.mult)
                        e.tensor_tensor(sch(rotB[:, :], HEADS)[:, :, 32:64], qv[:, :, 0:32],
                                        tab(teff, TQ2, t)[:, 32:64].unsqueeze(1).broadcast_to([128, HEADS, 32]),
                                        ALU.mult)
                        e.drain()
                        self.sV.inc(e.tensor_tensor(qtok[:, t * D:(t + 1) * D],
                                                    rotA[:, :], rotB[:, :], ALU.add))
                    blk.vector(vrotq)

                    def vrotk(e, t=t):
                        kv = sch(pp[:, D:2 * D], HEADS)
                        e.tensor_tensor(sch(rotA[:, :], HEADS), kv,
                                        tab(teff, TK1, t).unsqueeze(1).broadcast_to([128, HEADS, DH]),
                                        ALU.mult)
                        e.tensor_tensor(sch(rotB[:, :], HEADS)[:, :, 0:32], kv[:, :, 32:64],
                                        tab(teff, TK2, t)[:, 0:32].unsqueeze(1).broadcast_to([128, HEADS, 32]),
                                        ALU.mult)
                        e.tensor_tensor(sch(rotB[:, :], HEADS)[:, :, 32:64], kv[:, :, 0:32],
                                        tab(teff, TK2, t)[:, 32:64].unsqueeze(1).broadcast_to([128, HEADS, 32]),
                                        ALU.mult)
                        e.drain()
                        e.tensor_tensor(rotA[:, :], rotA[:, :], rotB[:, :], ALU.add)
                        e.drain()
                        self.sV.inc(e.tensor_tensor(
                            sch(ktok[:, :], HEADS), sch(rotA[:, :], HEADS),
                            kst[:, :].unsqueeze(2).broadcast_to([128, HEADS, DH]), ALU.mult))
                    blk.vector(vrotk)

                    def av5(e, slot=slot):
                        e.wait_ge(self.sV.h, self.sV.n)
                        self.sA.inc(e.activation(v_roll[:, slot * D:(slot + 1) * D],
                                                 pp[:, 2 * D:3 * D], AF.Copy))
                    blk.scalar(av5)

                    for half, (srcbuf, soff) in enumerate(((qtok, t * D), (ktok, 0))):
                        def pe2(e, half=half, srcbuf=srcbuf, soff=soff):
                            e.wait_ge(self.sV.h, self.sV.n)
                            if half == 0 and t == 0:
                                pass
                            for j in range(NCH):
                                i_ = e.matmul(ptr[:, j * 128:(j + 1) * 128],
                                              srcbuf[:, soff + j * 128:soff + (j + 1) * 128],
                                              self.ident[:, :], is_transpose=True,
                                              start=(j == 0), stop=(j == NCH - 1))
                            self.sP.inc(i_)
                        blk.tensor(pe2)

                        def vt2(e, half=half, t=t):
                            e.wait_ge(self.sP.h, self.sP.n)
                            if half == 0:
                                dst = sch(qTw[:, :], NCH)[:, :, t * 128:(t + 1) * 128]
                            else:
                                dst = sch(kT_roll[:, :], NCH)[:, :, 256 + t * 128:256 + (t + 1) * 128]
                            self.sV.inc(e.tensor_copy(dst, sch(ptr[:, :], NCH)))
                        blk.vector(vt2)

                if w == 0 and "m5" in self.debug:
                    self.tap5(blk, "qtok", qtok, [128, 4 * D], BF16)
                    self.tap5(blk, "vroll", v_roll, [128, 6 * D], BF16)
                    self.tap5(blk, "ktr", kT_roll, [128, NCH * 768], BF16)
                    self.tap5(blk, "qtw", qTw, [128, NCH * WA], BF16)
                    self.tap5(blk, "stats", stats, [128, 4 * HEADS], F32)

                for hp in range(HEADS // 2):
                    for sub in range(2):
                        h = 2 * hp + sub
                        pbase = 64 * sub
                        for t in range(4):
                            g = w * 4 + t
                            variant = 0 if g == 0 else (1 if g == 1 else 2)
                            bank = t % 2

                            def pe(e, h=h, t=t, hp=hp, pbase=pbase, bank=bank):
                                if h == 0 and t == 0:
                                    e.wait_ge(self.sV.h, self.sV.n)
                                if t >= 2 or h > 0 or hp > 0:
                                    e.wait_ge(self.sA.h, self.sA.n)
                                self.sP.inc(e.matmul(
                                    pp[:, bank * 512:bank * 512 + 384],
                                    qTw[pbase:pbase + 64, (h // 2) * WA + t * 128:(h // 2) * WA + (t + 1) * 128],
                                    kT_roll[pbase:pbase + 64, (h // 2) * 768 + t * 128:(h // 2) * 768 + t * 128 + 384],
                                    start=True, stop=True))
                            blk.tensor(pe)

                            def vmk(e, variant=variant, bank=bank):
                                e.wait_ge(self.sP.h, self.sP.n)
                                self.sV.inc(e.scalar_tensor_tensor(
                                    tmsk[:, :], pp[:, bank * 512:bank * 512 + 384], 1.0,
                                    self.masks[:, variant * 384:(variant + 1) * 384],
                                    ALU.mult, ALU.add))
                            blk.vector(vmk)

                            def aex(e, h=h, t=t):
                                e.wait_ge(self.sV.h, self.sV.n)
                                e.activation(
                                    p5p[:, :], tmsk[:, :], AF.Exp,
                                    scale=stats[:, t * HEADS + h:t * HEADS + h + 1],
                                    bias=self.negc128[:, l:l + 1],
                                    accum_out=rs5[:, :])
                                self.sA.inc(e.drain())
                            blk.scalar(aex)

                            def vnm(e, h=h, t=t, w=w):
                                e.wait_ge(self.sA.h, self.sA.n)
                                if w == 0 and h == 8 and t == 0 and "m5" in self.debug:
                                    e.tensor_copy(dbs[:, 0:384], tmsk[:, :])
                                    e.tensor_copy(dbs[:, 384:768], p5p[:, :])
                                    e.tensor_copy(dbs[:, 768:769], rs5[:, :])
                                e.reciprocal(rc5[:, :], rs5[:, :])
                                e.drain()
                                self.sV.inc(e.tensor_scalar(p5p[:, :], p5p[:, :],
                                                            rc5[:, :], None, ALU.mult))
                            blk.vector(vnm)

                            def pe3(e, t=t):
                                e.wait_ge(self.sV.h, self.sV.n)
                                for j in range(3):
                                    i_ = e.matmul(ptr[:, j * 128:(j + 1) * 128],
                                                  p5p[:, j * 128:(j + 1) * 128],
                                                  self.ident[:, :], is_transpose=True,
                                                  start=(j == 0), stop=(j == 2))
                                self.sP.inc(i_)
                            blk.tensor(pe3)

                            def vt3(e, t=t):
                                e.wait_ge(self.sP.h, self.sP.n)
                                dst = sch(pT_h[:, :], 6)[:, t:t + 3, t * 128:(t + 1) * 128]
                                self.sV.inc(e.tensor_copy(dst, sch(ptr[:, 0:384], 3)))
                            blk.vector(vt3)

                        def pe4(e, h=h, w=w, pbase=pbase):
                            e.wait_ge(self.sV.h, self.sV.n)
                            g0 = w * 4 - 2
                            for kc in range(6):
                                slot_ = (g0 + kc) % 6
                                i_ = e.matmul(pp[pbase:pbase + 64, 1024:1536],
                                              v_roll[:, slot_ * D + h * DH:slot_ * D + (h + 1) * DH],
                                              pT_h[:, kc * WA:(kc + 1) * WA],
                                              start=(kc == 0), stop=(kc == 5),
                                              tile_position=(0, pbase))
                            self.sP.inc(i_)
                        blk.tensor(pe4)

                    def avp(e, hp=hp):
                        e.wait_ge(self.sP.h, self.sP.n)
                        self.sA.inc(e.activation(aoT[:, hp * WA:(hp + 1) * WA],
                                                 pp[:, 1024:1536], AF.Copy))
                    blk.scalar(avp)

                if w == 0 and "m5" in self.debug:
                    self.tap5(blk, "aot", aoT, [128, NCH * WA], BF16)
                    self.tap5(blk, "pth", pT_h, [128, 6 * WA], BF16)
                    self.tap5(blk, "dbs", dbs, [128, 1536], F32)

                for ec in range(NCH):
                    bank = ec % 2

                    def pe(e, ec=ec, bank=bank):
                        e.wait_ge(self.sA.h, self.sA.n)
                        if ec >= 2:
                            e.wait_ge(self.sV.h, self.sV.n)
                        for dc in range(NCH):
                            i_ = e.matmul(pp[:, 1536 + bank * 512:1536 + bank * 512 + 512],
                                          wom[:, dc * D + ec * 128:dc * D + (ec + 1) * 128],
                                          aoT[:, dc * WA:(dc + 1) * WA],
                                          start=(dc == 0), stop=(dc == NCH - 1))
                        self.sP.inc(i_)
                    blk.tensor(pe)

                    def vres(e, ec=ec, bank=bank):
                        e.wait_ge(self.sP.h, self.sP.n)
                        if ec < 2:
                            e.wait_ge(self.sO.h, self.sO.n)
                        self.sV.inc(e.scalar_tensor_tensor(
                            x_win[:, ec * 513 + 1:ec * 513 + 513],
                            pp[:, 1536 + bank * 512:1536 + bank * 512 + 512], 1.0,
                            x_win[:, ec * 513 + 1:ec * 513 + 513], ALU.mult, ALU.add))
                    blk.vector(vres)

                def st(e, w0=w0):
                    e.wait_ge(self.sV.h, self.sV.n)
                    e.dma_start(dch(self.xmB.ap()[:, w0:w0 + WA]),
                                sch(x_win[:, :], NCH)[:, :, 1:513]).then_inc(self.sO.h, 16)
                    self.sO.n += 16
                blk.sync(st)
        es.close()

    def _p5b_ffn(self, l):
        nc = self.nc
        pvl = PV_L0 + l * PV_PER_L
        from contextlib import ExitStack
        es = ExitStack()
        sb = lambda n_, sh_, dt_: es.enter_context(nc.sbuf_tensor(n_ + f"_{l}b", sh_, dt_))
        x_win = sb("xwinf", [128, NCH * 1025], F32)
        xsh_b = sb("xshb", [128, NCH * LW], BF16)
        x2b = sb("x2b", [128, NCH * LW], BF16)
        xn_bf = sb("xnbf", [128, NCH * WF], BF16)
        xtmp = sb("xtmpf", [128, WF], F32)
        gated = sb("gatedf", [128, NPAIR * WF], BF16)
        gsb = sb("gsbf", [128, WF], BF16)
        w1r = sb("w1rf", [128, 2 * NCH * 256], BF16)
        w2r = sb("w2rf", [128, 2 * NPAIR * 128], BF16)
        st5 = sb("st5f", [1, 3 * LW], F32)
        arow = sb("arowf", [1, S], F32)
        brow5 = sb("brow5f", [1, S], F32)
        pf = es.enter_context(nc.psum_tensor(f"pf_{l}b", [128, 3 * 1024], F32))
        pst = es.enter_context(nc.psum_tensor(f"pst_{l}b", [1, 1024], F32))

        for w in range(S // LW):
            w0 = w * LW
            with nc.Block() as blk:
                def ld(e, w=w, w0=w0):
                    if w == 0:
                        e.wait_ge(self.sO.h, self.sO.n)
                        e.dma_start(sch(x_win[:, 0:NCH * 513], NCH)[:, :, 1:513],
                                    dch(self.xmB.ap()[:, 0:LW])).then_inc(self.sD.h, 16)
                    else:
                        e.dma_start(sch(x_win[:, 0:NCH * 513], NCH),
                                    dch(self.xmB.ap()[:, w0 - 1:w0 + LW])).then_inc(self.sD.h, 16)
                    self.sD.n += 16
                blk.sync(ld)

                def vsh(e, w=w):
                    e.wait_ge(self.sD.h, self.sD.n)
                    if w == 0:
                        e.memset(sch(x_win[:, 0:NCH * 513], NCH)[:, :, 0:1], 0.0)
                    e.tensor_copy(sch(xsh_b[:, :], NCH)[:, 0:4, :],
                                  sch(x_win[:, 0:NCH * 513], NCH)[:, 0:4, 1:513])
                    self.sV.inc(e.tensor_copy(sch(xsh_b[:, :], NCH)[:, 4:NCH, :],
                                              sch(x_win[:, 0:NCH * 513], NCH)[:, 4:NCH, 0:512]))
                blk.vector(vsh)

                def pe(e):
                    e.wait_ge(self.sV.h, self.sV.n)
                    i_ = None
                    for c in range(NCH):
                        i_ = e.matmul(pst[0:1, 0:LW], self.ones_k[:, :],
                                      xsh_b[:, c * LW:(c + 1) * LW],
                                      start=(c == 0), stop=(c == NCH - 1))
                    self.sP.inc(i_)
                blk.tensor(pe)

                def ac(e):
                    e.wait_ge(self.sP.h, self.sP.n)
                    e.wait_ge(self.sV.h, self.sV.n)
                    self.sA.inc(e.activation(st5[0:1, 0:LW], pst[0:1, 0:LW], AF.Copy,
                                             scale=-1.0 / D))
                blk.scalar(ac)

                def pe_b(e):
                    e.wait_ge(self.sA.h, self.sA.n)
                    self.sP.inc(e.matmul(pf[:, 0:LW], self.ones_mf[:, :],
                                         st5[0:1, 0:LW], start=True, stop=True))
                blk.tensor(pe_b)

                def vc(e, w=w):
                    e.wait_ge(self.sP.h, self.sP.n)
                    for c in range(NCH):
                        off = c * 513 + (1 if c < 4 else 0)
                        i_ = e.tensor_tensor(xsh_b[:, c * LW:(c + 1) * LW],
                                             pf[:, 0:LW],
                                             x_win[:, off:off + LW], ALU.add)
                    self.sV.inc(i_)
                blk.vector(vc)

                def asq(e):
                    e.wait_ge(self.sV.h, self.sV.n)
                    self.sA.inc(e.activation(x2b[:, :], xsh_b[:, :], AF.Square))
                blk.scalar(asq)

                def pe2s(e):
                    e.wait_ge(self.sA.h, self.sA.n)
                    i_ = None
                    for c in range(NCH):
                        i_ = e.matmul(pst[0:1, LW:2 * LW], self.ones_k[:, :],
                                      x2b[:, c * LW:(c + 1) * LW],
                                      start=(c == 0), stop=(c == NCH - 1))
                    self.sP.inc(i_)
                blk.tensor(pe2s)

                def ac2(e):
                    e.wait_ge(self.sP.h, self.sP.n)
                    self.sA.inc(e.activation(st5[0:1, LW:2 * LW], pst[0:1, LW:2 * LW],
                                             AF.Copy, scale=1.0 / D))
                blk.scalar(ac2)

                def aa(e, w0=w0):
                    e.wait_ge(self.sA.h, self.sA.n)
                    e.drain()
                    e.activation(arow[0:1, w0:w0 + LW], st5[0:1, LW:2 * LW], AF.Ln,
                                 bias=self.eps5[0:1, :])
                    e.drain()
                    self.sA.inc(e.activation(arow[0:1, w0:w0 + LW], arow[0:1, w0:w0 + LW],
                                             AF.Exp, scale=-0.5))
                blk.scalar(aa)

                def vb(e, w0=w0):
                    e.wait_ge(self.sA.h, self.sA.n)
                    self.sV.inc(e.tensor_tensor(
                        brow5[0:1, w0:w0 + LW], st5[0:1, 0:LW],
                        arow[0:1, w0:w0 + LW], ALU.mult))
                blk.vector(vb)

        for w in range(S // WF):
            w0 = w * WF
            with nc.Block() as blk:
                def ld(e, w=w, w0=w0):
                    if w == 0:
                        e.dma_start(sch(x_win[:, :], NCH)[:, :, 1:1025],
                                    dch(self.xmB.ap()[:, 0:WF])).then_inc(self.sD.h, 16)
                    else:
                        e.dma_start(sch(x_win[:, :], NCH),
                                    dch(self.xmB.ap()[:, w0 - 1:w0 + WF])).then_inc(self.sD.h, 16)
                    self.sD.n += 16
                blk.sync(ld)

                def pe(e, w=w, w0=w0):
                    e.wait_ge(self.sD.h, self.sD.n)
                    e.wait_ge(self.sV.h, self.sV.n)
                    for hh in range(WF // 512):
                        e.matmul(pf[:, hh * 512:(hh + 1) * 512], self.ones_mf[:, :],
                                 arow[0:1, w0 + hh * 512:w0 + (hh + 1) * 512],
                                 start=True, stop=True)
                        i_ = e.matmul(pf[:, 1024 + hh * 512:1024 + (hh + 1) * 512],
                                      self.ones_mf[:, :],
                                      brow5[0:1, w0 + hh * 512:w0 + (hh + 1) * 512],
                                      start=True, stop=True)
                    self.sP.inc(i_)
                blk.tensor(pe)

                for c in range(NCH):
                    def vxn(e, c=c, w=w):
                        e.wait_ge(self.sP.h, self.sP.n)
                        e.wait_ge(self.sA.h, self.sA.n)
                        if c == 0:
                            e.memset(sch(x_win[:, :], NCH)[:, :, 0:1], 0.0) if w == 0 else None
                        xs = x_win[:, c * 1025 + (1 if c < 4 else 0):
                                   c * 1025 + (1 if c < 4 else 0) + WF]
                        e.tensor_tensor(xtmp[:, :], pf[:, 0:WF], xs, ALU.mult)
                        e.drain()
                        self.sV.inc(e.tensor_tensor(xtmp[:, :], pf[:, 1024:1024 + WF],
                                                    xtmp[:, :], ALU.add))
                    blk.vector(vxn)

                    def axn(e, c=c):
                        e.wait_ge(self.sV.h, self.sV.n)
                        self.sA.inc(e.activation(
                            xn_bf[:, c * WF:(c + 1) * WF], xtmp[:, :], AF.Identity,
                            scale=self.pvec[:, pvl + PV_LNG + c:pvl + PV_LNG + c + 1],
                            bias=self.pvec[:, pvl + PV_LNB + c:pvl + PV_LNB + c + 1]))
                    blk.scalar(axn)

                if w == 0 and "m5b" in self.debug:
                    self.tap5(blk, "xnbf", xn_bf, [128, NCH * WF], BF16)
                    self.tap5(blk, "arow", arow, [1, S], F32)
                    self.tap5(blk, "brow5", brow5, [1, S], F32)

                for i in range(NPAIR):
                    def ldw(e, i=i, half=0):
                        if i >= 2:
                            e.wait_ge(self.sP.h, self.sP.n)
                        e.dma_start(sch(w1r[:, (i % 2) * NCH * 256:(i % 2 + 1) * NCH * 256], NCH),
                                    dch(self.w1_b.ap()[l][:, 256 * i:256 * (i + 1)])
                                    ).then_inc(self.sD.h, 16)
                        self.sD.n += 16
                    blk.sync(ldw)

                    def pe1(e, i=i):
                        e.wait_ge(self.sD.h, self.sD.n)
                        e.wait_ge(self.sA.h, self.sA.n)
                        if i >= 1:
                            e.wait_ge(self.sV.h, self.sV.n)
                        wb = w1r[:, (i % 2) * NCH * 256:]
                        for half in range(2):
                            for hh in range(WF // 512):
                                dst = pf[:, half * 1024 + hh * 512:half * 1024 + (hh + 1) * 512]
                                for dc in range(NCH):
                                    i_ = e.matmul(dst,
                                                  wb[:, dc * 256 + half * 128:dc * 256 + half * 128 + 128],
                                                  xn_bf[:, dc * WF + hh * 512:dc * WF + (hh + 1) * 512],
                                                  start=(dc == 0), stop=(dc == NCH - 1))
                        self.sP.inc(i_)
                    blk.tensor(pe1)

                    def ag(e, i=i):
                        e.wait_ge(self.sP.h, self.sP.n)
                        e.wait_ge(self.sV.h, self.sV.n)
                        self.sA.inc(e.activation(
                            gsb[:, :], pf[:, 1024:1024 + WF], AF.Gelu,
                            bias=self.pvec[:, pvl + PV_B1 + 2 * i + 1:pvl + PV_B1 + 2 * i + 2]))
                    blk.scalar(ag)

                    def vg(e, i=i):
                        e.wait_ge(self.sA.h, self.sA.n)
                        self.sV.inc(e.scalar_tensor_tensor(
                            gated[:, i * WF:(i + 1) * WF], pf[:, 0:WF],
                            self.pvec[:, pvl + PV_B1 + 2 * i:pvl + PV_B1 + 2 * i + 1],
                            gsb[:, :], ALU.add, ALU.mult))
                    blk.vector(vg)

                if w == 0 and "m5b" in self.debug:
                    self.tap5(blk, "gated", gated, [128, NPAIR * WF], BF16)

                for ec in range(NCH):
                    def ldw2(e, ec=ec):
                        if ec >= 2:
                            e.wait_ge(self.sP.h, self.sP.n)
                        e.dma_start(sch(w2r[:, (ec % 2) * NPAIR * 128:(ec % 2 + 1) * NPAIR * 128], NPAIR),
                                    dch(self.w2_b.ap()[l][:, 128 * ec:128 * (ec + 1)])
                                    ).then_inc(self.sD.h, 16)
                        self.sD.n += 16
                    blk.sync(ldw2)

                    def pe2(e, ec=ec):
                        e.wait_ge(self.sD.h, self.sD.n)
                        e.wait_ge(self.sV.h, self.sV.n)
                        wb = w2r[:, (ec % 2) * NPAIR * 128:]
                        for hh in range(WF // 512):
                            dst = pf[:, 2048 + hh * 512:2048 + (hh + 1) * 512]
                            for ic in range(NPAIR):
                                i_ = e.matmul(dst, wb[:, ic * 128:(ic + 1) * 128],
                                              gated[:, ic * WF + hh * 512:ic * WF + (hh + 1) * 512],
                                              start=(ic == 0), stop=(ic == NPAIR - 1))
                        self.sP.inc(i_)
                    blk.tensor(pe2)

                    def vres(e, ec=ec):
                        e.wait_ge(self.sP.h, self.sP.n)
                        if ec < 2:
                            e.wait_ge(self.sO.h, self.sO.n)
                        self.sV.inc(e.scalar_tensor_tensor(
                            x_win[:, ec * 1025 + 1:ec * 1025 + 1025],
                            pf[:, 2048:2048 + WF],
                            self.pvec[:, pvl + PV_B2 + ec:pvl + PV_B2 + ec + 1],
                            x_win[:, ec * 1025 + 1:ec * 1025 + 1025], ALU.add, ALU.add))
                    blk.vector(vres)

                def st(e, w0=w0):
                    e.wait_ge(self.sV.h, self.sV.n)
                    e.dma_start(dch(self.xmA.ap()[:, w0:w0 + WF]),
                                sch(x_win[:, :], NCH)[:, :, 1:1025]).then_inc(self.sO.h, 16)
                    self.sO.n += 16
                blk.sync(st)
        es.close()

    def p6_final(self):
        nc = self.nc
        from contextlib import ExitStack
        es = ExitStack()
        g_win = es.enter_context(nc.sbuf_tensor("g6", [128, NCH * LW], F32))
        s_win = es.enter_context(nc.sbuf_tensor("s6", [128, NCH * LW], F32))
        xw6 = es.enter_context(nc.sbuf_tensor("xw6", [128, NCH * LW], F32))
        for w in range(S // LW):
            w0 = w * LW
            with nc.Block() as blk:
                def ld(e, w=w, w0=w0):
                    if w == 0:
                        e.wait_ge(self.sO.h, self.sO.n)
                    e.dma_start(sch(g_win[:, :], NCH),
                                dch(self.xmA.ap()[:, w0:w0 + LW])).then_inc(self.sD.h, 16)
                    e.dma_start(sch(s_win[:, :], NCH),
                                dch(self.ssc.ap()[:, w0:w0 + LW])).then_inc(self.sD.h, 16)
                    e.dma_start(sch(xw6[:, :], NCH),
                                dch(self.xT.ap()[:, w0:w0 + LW])).then_inc(self.sD.h, 16)
                    self.sD.n += 48
                blk.sync(ld)

                def ac(e):
                    e.wait_ge(self.sD.h, self.sD.n)
                    self.sA.inc(e.activation(g_win[:, :], g_win[:, :], AF.Silu))
                blk.scalar(ac)

                def ve(e):
                    e.wait_ge(self.sA.h, self.sA.n)
                    e.tensor_tensor(s_win[:, :], s_win[:, :], g_win[:, :], ALU.mult)
                    e.drain()
                    self.sV.inc(e.tensor_tensor(xw6[:, :], xw6[:, :], s_win[:, :], ALU.add))
                blk.vector(ve)

                def st(e, w0=w0):
                    e.wait_ge(self.sV.h, self.sV.n)
                    e.dma_start(dch(self.outT.ap()[:, w0:w0 + LW]),
                                sch(xw6[:, :], NCH)).then_inc(self.sO.h, 16)
                    self.sO.n += 16
                blk.sync(st)
        es.close()

    def finish(self):
        nc = self.nc
        dram_taps = {
            "xn_b": (self.xn_b, [D, S], BF16),
            "xmA": (self.xmA, [D, S], F32),
            "xmB": (self.xmB, [D, S], F32),
            "vsc": (self.vsc, [S, D], BF16),
            "ssc": (self.ssc, [D, S], F32),
            "wqkv_b": (self.wqkv_b, [NL, D, 3 * D], BF16),
            "wvT_b": (self.wvT_b, [D, D], BF16),
            "wkT_b": (self.wkT_b, [D, D], BF16),
        }
        taps = [t for t in self.debug if t in dram_taps]
        if taps:
            with nc.Block() as blk:
                def dump(e):
                    e.wait_ge(self.sO.h, self.sO.n)
                    for t in taps:
                        h, shape, dt = dram_taps[t]
                        o = self.nc.dram_tensor("dbg_" + t, shape, dt, kind="ExternalOutput")
                        ap = h.ap()
                        e.dma_start(o.ap()[:, :] if len(shape) == 2 else o.ap()[:, :, :],
                                    ap[:, :] if len(shape) == 2 else ap[:, :, :]
                                    ).then_inc(self.sO.h, 16)
                        self.sO.n += 16
                blk.sync(dump)
        with nc.Block() as blk:
            def fin(e):
                e.wait_ge(self.sO.h, self.sO.n)
            blk.sync(fin)


def build(debug=(), phases=("p0", "p1", "p2")):
    nc = bass.Bass("TRN2", target_bir_lowering=False)
    kb = KB(nc, debug=debug)
    if "p0" in phases:
        kb.p0_setup()
    if "p1" in phases:
        kb.p1_cast_weights()
    if "p2" in phases:
        kb.p2_ln()
    if "p3" in phases:
        kb.p34_outer()
    if "p5" in phases:
        kb.p5_mini()
    if "p6" in phases:
        kb.p6_final()
    kb.finish()
    nc.finalize()
    return nc, kb


def kernel(**inputs):
    sh = host_shared(inputs)
    x = np.asarray(inputs["L0_attn_x"], np.float32)
    nc, kb = build(phases=("p0", "p1", "p2", "p3", "p5", "p6"))
    in_maps = []
    for c in range(8):
        m = dict(sh)
        m["xT"] = np.ascontiguousarray(x[c].T)
        in_maps.append(m)
    res = run_bass_kernel_spmd(nc, in_maps, core_ids=list(range(8)))
    out = np.stack([np.asarray(r["outT"], np.float32).T for r in res.results], 0)
    return out


# version 04
# speedup vs baseline: 1.1227x; 1.1227x over previous
import numpy as np
import ml_dtypes
import concourse.bass as bass
import concourse.mybir as mybir
from concourse.bass_utils import run_bass_kernel_spmd

F32 = mybir.dt.float32
BF16 = mybir.dt.bfloat16
AF = mybir.ActivationFunctionType
ALU = mybir.AluOpType
AX = mybir.AxisListType

D = 1024
S = 4096
NCH = 8
NL = 6
LW = 512
WA = 512
WF = 1024
HEADS = 16
DH = 64
WIN = 256
DI2 = 5632
NPAIR = 22
DIP = 2816
NEG = -1.0e30

PV_BQ = 0
PV_BK = 8
PV_BO = 16
PV_L0 = 24
PV_PER_L = 68
PV_LNG = 0
PV_LNB = 8
PV_B2 = 16
PV_B1 = 24
NPV = PV_L0 + NL * PV_PER_L


def _host_tables():
    half = np.arange(0, DH, 2, dtype=np.float32)
    inv_freq = (1.0 / 10000.0 ** (half / DH)).astype(np.float32)
    t = np.arange(S, dtype=np.float32)
    freqs = t[:, None] * inv_freq[None, :]
    freqs = np.concatenate([freqs, freqs], -1)
    base_scale = ((half + 0.4 * DH) / (1.4 * DH)).astype(np.float32)
    power = (t - S // 2) / (WIN // 2)
    scale = base_scale[None, :] ** power[:, None]
    scale = np.concatenate([scale, scale], -1).astype(np.float32)
    cos = np.cos(freqs).astype(np.float32)
    sin = np.sin(freqs).astype(np.float32)
    inv = (1.0 / scale).astype(np.float32)
    sign = np.concatenate([-np.ones(DH // 2, np.float32), np.ones(DH // 2, np.float32)])
    t1q = cos * scale
    t2q = sign[None, :] * sin * scale
    t1k = cos * inv
    t2k = sign[None, :] * sin * inv
    return (np.ascontiguousarray(a) for a in (t1q, t2q, t1k, t2k))


def _host_masks():
    m = np.zeros((3, 128, 384), np.float32)
    p = np.arange(128)[:, None]
    c = np.arange(384)[None, :]
    allowed = (c >= p) & (c <= p + 256)
    for v, cut in ((0, 256), (1, 128), (2, 0)):
        a = allowed & (c >= cut)
        m[v] = np.where(a, 0.0, NEG)
    return m


def _pack_pvec(ins):
    pv = np.zeros((128, NPV), np.float32)

    def put(col0, vec):
        v = np.asarray(vec, np.float32).reshape(-1)
        n = v.shape[0] // 128
        pv[:, col0:col0 + n] = v.reshape(n, 128).T

    put(PV_BQ, ins["L0_attn_bq"])
    put(PV_BK, ins["L0_attn_bk"])
    put(PV_BO, ins["L0_attn_bo"])
    for l in range(NL):
        base = PV_L0 + l * PV_PER_L
        put(base + PV_LNG, ins["L0_ffn_ln_g"][l])
        put(base + PV_LNB, ins["L0_ffn_ln_b"][l])
        put(base + PV_B2, ins["L0_ffn_b2"][l])
        b1 = np.asarray(ins["L0_ffn_b1"][l], np.float32)
        b1p = _permute_w1_cols(b1[None, :])[0]
        put(base + PV_B1, b1p)
    return pv


def _permute_w1_cols(w1):
    lead = w1.shape[:-1]
    a = w1[..., :2730]
    g = w1[..., 2730:]
    out = np.zeros(lead + (DI2,), w1.dtype)
    for i in range(NPAIR):
        lo, hi = 128 * i, min(128 * (i + 1), 2730)
        n = hi - lo
        out[..., 256 * i:256 * i + n] = a[..., lo:hi]
        out[..., 256 * i + 128:256 * i + 128 + n] = g[..., lo:hi]
    return out


def _pad_w2_rows(w2):
    out = np.zeros((w2.shape[0], DIP, w2.shape[2]), w2.dtype)
    out[:, :2730, :] = w2
    return out


_HOST_CACHE = {}


def host_shared(ins):
    key = id(ins.get("L0_attn_Wq"))
    t1q, t2q, t1k, t2k = _host_tables()
    sh = {
        "wqT": np.ascontiguousarray(np.asarray(ins["L0_attn_Wq"], np.float32).T),
        "wkT": np.ascontiguousarray(np.asarray(ins["L0_attn_Wk"], np.float32).T),
        "wvT": np.ascontiguousarray(np.asarray(ins["L0_attn_Wv"], np.float32).T),
        "woT": np.ascontiguousarray(np.asarray(ins["L0_attn_Wo"], np.float32).T),
        "wqkv": np.ascontiguousarray(np.asarray(ins["L0_ffn_Wqkv"], np.float32)),
        "wom": np.ascontiguousarray(np.asarray(ins["L0_ffn_Wo"], np.float32)),
        "w1p": np.ascontiguousarray(_permute_w1_cols(np.asarray(ins["L0_ffn_W1"], np.float32))),
        "w2p": np.ascontiguousarray(_pad_w2_rows(np.asarray(ins["L0_ffn_W2"], np.float32))),
        "pvec": _pack_pvec(ins),
        "brow": np.asarray(ins["L0_attn_bv"], np.float32).reshape(1, D),
        "qsks": np.concatenate(
            [np.asarray(ins["L0_ffn_q_scale"], np.float32),
             np.asarray(ins["L0_ffn_k_scale"], np.float32)], axis=1),
        "t1q": t1q, "t2q": t2q, "t1k": t1k, "t2k": t2k,
        "lmask": _host_masks(),
    }
    return sh


def dch(ap2d, p=128):
    return ap2d.rearrange("(c p) n -> p c n", p=p)


def sch(ap2d, c):
    return ap2d.rearrange("p (c n) -> p c n", c=c)


class Ctr:
    def __init__(self, h):
        self.h = h
        self.n = 0

    def inc(self, instr, k=1):
        instr.then_inc(self.h, k)
        self.n += k


class KB:
    def __init__(self, nc, debug=()):
        self.nc = nc
        self.debug = set(debug)
        self.dbg_outs = {}
        n = nc

        def din(name, shape, dt=F32):
            return n.dram_tensor(name, shape, dt, kind="ExternalInput")

        self.xT = din("xT", [D, S])
        self.wqT = din("wqT", [D, D]); self.wkT = din("wkT", [D, D])
        self.wvT = din("wvT", [D, D]); self.woT = din("woT", [D, D])
        self.wqkv = din("wqkv", [NL, D, 3 * D])
        self.wom = din("wom", [NL, D, D])
        self.w1p = din("w1p", [NL, D, DI2])
        self.w2p = din("w2p", [NL, DIP, D])
        self.pvec_d = din("pvec", [128, NPV])
        self.brow_d = din("brow", [1, D])
        self.qsks_d = din("qsks", [NL, 128])
        self.t1q_d = din("t1q", [S, DH]); self.t2q_d = din("t2q", [S, DH])
        self.t1k_d = din("t1k", [S, DH]); self.t2k_d = din("t2k", [S, DH])
        self.lmask_d = din("lmask", [3, 128, 384])

        self.outT = n.dram_tensor("outT", [D, S], F32, kind="ExternalOutput")

        def sc(name, shape, dt):
            return n.dram_tensor(name, shape, dt, kind="Internal")

        self.xn_b = sc("xn_b", [D, S], BF16)
        self.xmA = sc("xmA", [D, S], F32)
        self.xmB = sc("xmB", [D, S], F32)
        self.vsc = sc("vsc", [S, D], BF16)
        self.ssc = sc("ssc", [D, S], F32)
        self.wqT_b = sc("wqT_b", [D, D], BF16); self.wkT_b = sc("wkT_b", [D, D], BF16)
        self.wvT_b = sc("wvT_b", [D, D], BF16); self.woT_b = sc("woT_b", [D, D], BF16)
        self.wqkv_b = sc("wqkv_b", [NL, D, 3 * D], BF16)
        self.wom_b = sc("wom_b", [NL, D, D], BF16)
        self.w1_b = sc("w1_b", [NL, D, DI2], BF16)
        self.w2_b = sc("w2_b", [NL, DIP, D], BF16)
        self.stat_sc = sc("stat_sc", [64, 2], F32)

        ident_np = np.eye(128).astype(ml_dtypes.bfloat16)
        self.ident_d = n.inline_tensor(ident_np, "ident")

        self._es_global = []
        self.sD = Ctr(self._sem("sD"))
        self.sO = Ctr(self._sem("sO"))
        self.sP = Ctr(self._sem("sP"))
        self.sV = Ctr(self._sem("sV"))
        self.sA = Ctr(self._sem("sA"))
        self.sG = Ctr(self._sem("sG"))

        self.pvec = self._sb("pvec", [128, NPV], F32)
        self.brow = self._sb("brow", [1, D], F32)
        self.brow_bf = self._sb("brow_bf", [1, D], BF16)
        self.qsks = self._sb("qsks", [NL, 128], F32)
        self.ident = self._sb("ident", [128, 128], BF16)
        self.masks = self._sb("masks", [128, 3 * 384], F32)
        self.ones_k = self._sb("ones_k", [128, 1], BF16)
        self.ones_m = self._sb("ones_m", [1, 128], BF16)
        self.ones_mf = self._sb("ones_mf", [1, 128], F32)
        self.negc128 = self._sb("negc128", [128, NL], F32)
        self.eps24 = self._sb("eps24", [128, 1], F32)
        self.eps5 = self._sb("eps5", [1, 1], F32)
        self.tiny = self._sb("tinybuf", [128, 64], F32)
        self.snap4 = self._sb("snap4", [128, 320], F32)

    def _sem(self, name):
        cm = self.nc.semaphore(name)
        h = cm.__enter__()
        self._es_global.append(cm)
        return h

    def _sb(self, name, shape, dt):
        return self.nc.alloc_sbuf_tensor("sb_" + name, shape, dt)

    def tap5(self, blk, tag, buf, shape, dt):
        name = "dbg_" + tag
        if name in self.dbg_outs:
            return
        t = self.nc.dram_tensor(name, shape, dt, kind="ExternalOutput")
        self.dbg_outs[name] = t

        def f(e):
            e.wait_ge(self.sV.h, self.sV.n)
            e.wait_ge(self.sA.h, self.sA.n)
            e.wait_ge(self.sP.h, self.sP.n)
            e.dma_start(t.ap()[:, :], buf[:, 0:shape[1]]).then_inc(self.sO.h, 16)
        self.sO.n += 16
        blk.gpsimd(f)

    def tap(self, blk, tag, ap_fn, shape, dt=F32):
        if tag not in self.debug:
            return
        name = "dbg_" + tag
        if name not in self.dbg_outs:
            self.dbg_outs[name] = self.nc.dram_tensor(name, shape, dt, kind="ExternalOutput")
        t = self.dbg_outs[name]

        def f(e):
            e.dma_start(t.ap()[:, :], ap_fn()).then_inc(self.sO.h, 16)
        self.sO.n += 16
        blk.gpsimd(f)

    def p0_setup(self):
        nc = self.nc
        with nc.Block() as blk:
            def loads(e):
                e.dma_start(self.pvec[:, :], self.pvec_d.ap()[:, :]).then_inc(self.sD.h, 16)
                e.dma_start(self.brow[:, :], self.brow_d.ap()[:, :]).then_inc(self.sD.h, 16)
                e.dma_start(self.qsks[:, :], self.qsks_d.ap()[:, :]).then_inc(self.sD.h, 16)
                e.dma_start(self.ident[:, :], self.ident_d.ap()[:, :]).then_inc(self.sD.h, 16)
                e.dma_start(self.masks[:, :].rearrange("p (v n) -> p v n", v=3),
                            self.lmask_d.ap()[:, :, :].rearrange("v p n -> p v n")
                            ).then_inc(self.sD.h, 16)
                self.sD.n += 5 * 16
                e.wait_ge(self.sD.h, self.sD.n)
            blk.sync(loads)

            def vset(e):
                e.wait_ge(self.sD.h, self.sD.n)
                e.memset(self.ones_k[:, :], 1.0)
                e.memset(self.ones_m[:, :], 1.0)
                e.memset(self.ones_mf[:, :], 1.0)
                e.memset(self.eps24[:, :], 1e-24)
                e.memset(self.eps5[:, :], 1e-5)
                e.tensor_copy(self.brow_bf[:, :], self.brow[:, :])
            blk.vector(vset)

        with nc.Block() as blk, nc.psum_tensor("p0ps", [128, 16], F32) as pps:
            def v1(e):
                self.sV.inc(e.reduce_max(self.tiny[0:NL, 0:2],
                                         self.qsks[:, :].rearrange("l (h d) -> l h d", h=2),
                                         AX.X, apply_absolute_value=True))
            blk.vector(v1)

            def dmamove(e):
                e.wait_ge(self.sV.h, self.sV.n)
                e.dma_start(self.stat_sc.ap()[0:NL, :], self.tiny[0:NL, 0:2]
                            ).then_inc(self.sD.h, 16)
                self.sD.n += 16
                e.wait_ge(self.sD.h, self.sD.n)
                e.dma_start(self.tiny[0:1, 8:8 + 2 * NL],
                            self.stat_sc.ap()[0:NL, :].rearrange("l h -> (l h)"
                                                                 ).unsqueeze(0)
                            ).then_inc(self.sD.h, 16)
                self.sD.n += 16
            blk.sync(dmamove)

            def pe1(e):
                e.wait_ge(self.sD.h, self.sD.n)
                self.sP.inc(e.matmul(pps[:, 0:2 * NL], self.ones_mf[:, :],
                                     self.tiny[0:1, 8:8 + 2 * NL],
                                     start=True, stop=True))
            blk.tensor(pe1)

            def a2(e):
                e.wait_ge(self.sP.h, self.sP.n)
                self.sA.inc(e.activation(self.tiny[:, 8:8 + 2 * NL], pps[:, 0:2 * NL],
                                         AF.Copy))
            blk.scalar(a2)

            def v2(e):
                e.wait_ge(self.sA.h, self.sA.n)
                pairs = self.tiny[:, 8:8 + 2 * NL].rearrange("p (l h) -> p l h", l=NL)
                e.tensor_tensor(self.negc128[:, :], pairs[:, :, 0:1].squeeze(2),
                                pairs[:, :, 1:2].squeeze(2), ALU.mult)
                e.drain()
                self.sV.inc(e.tensor_scalar(self.negc128[:, :], self.negc128[:, :],
                                            -8.0, None, ALU.mult))
            blk.vector(v2)

    def p1_cast_weights(self):
        nc = self.nc
        jobs = []
        jobs.append((self.wqT.ap(), self.wqT_b.ap(), D, D))
        jobs.append((self.wkT.ap(), self.wkT_b.ap(), D, D))
        jobs.append((self.wvT.ap(), self.wvT_b.ap(), D, D))
        jobs.append((self.woT.ap(), self.woT_b.ap(), D, D))
        for l in range(NL):
            jobs.append((self.wqkv.ap()[l], self.wqkv_b.ap()[l], D, 3 * D))
            jobs.append((self.wom.ap()[l], self.wom_b.ap()[l], D, D))
            jobs.append((self.w1p.ap()[l], self.w1_b.ap()[l], D, DI2))
            jobs.append((self.w2p.ap()[l], self.w2_b.ap()[l], DIP, D))
        CMAX = 24576
        jobs2 = []
        for (s_, d_, rows, cols) in jobs:
            rc = rows // 128
            if rc * cols <= CMAX:
                jobs2.append((s_, d_, rows, cols))
            else:
                half = rows // 2
                jobs2.append((s_[0:half, :], d_[0:half, :], half, cols))
                jobs2.append((s_[half:rows, :], d_[half:rows, :], half, cols))
        with nc.sbuf_tensor("castf", [128, CMAX], F32) as cf, \
             nc.sbuf_tensor("castb", [128, CMAX], BF16) as cb:
            for (src, dst, rows, cols) in jobs2:
                rc = rows // 128
                assert rc * cols <= CMAX
                with nc.Block() as blk:
                    def ld(e, src=src, rc=rc, cols=cols):
                        e.dma_start(sch(cf[:, 0:rc * cols], rc),
                                    dch(src)).then_inc(self.sD.h, 16)
                        self.sD.n += 16
                    blk.sync(ld)

                    def cast(e, rc=rc, cols=cols):
                        e.wait_ge(self.sD.h, self.sD.n)
                        e.wait_ge(self.sO.h, self.sO.n)
                        self.sV.inc(e.tensor_copy(cb[:, 0:rc * cols], cf[:, 0:rc * cols]))
                    blk.vector(cast)

                    def st(e, dst=dst, rc=rc, cols=cols):
                        e.wait_ge(self.sV.h, self.sV.n)
                        e.dma_start(dch(dst),
                                    sch(cb[:, 0:rc * cols], rc)).then_inc(self.sO.h, 16)
                        self.sO.n += 16
                    blk.sync(st)

    def p2_ln(self, upto=99, nwin=None):
        nc = self.nc
        with nc.sbuf_tensor("xw", [128, NCH * LW], F32) as xw, \
             nc.sbuf_tensor("xbf", [128, NCH * LW], BF16) as xbf, \
             nc.sbuf_tensor("x2bf", [128, NCH * LW], BF16) as x2bf, \
             nc.sbuf_tensor("xnf", [128, NCH * LW], F32) as xnf, \
             nc.sbuf_tensor("xnb", [128, NCH * LW], BF16) as xnb, \
             nc.sbuf_tensor("st2", [1, 4 * LW], F32) as st2, \
             nc.psum_tensor("lps", [1, 2 * LW], F32) as lps, \
             nc.psum_tensor("lpb", [128, 2 * LW], F32) as lpb:
            for w in range(nwin or (S // LW)):
                w0 = w * LW
                with nc.Block() as blk:
                    def ld(e, w0=w0):
                        e.dma_start(sch(xw[:, :], NCH),
                                    dch(self.xT.ap()[:, w0:w0 + LW])
                                    ).then_inc(self.sD.h, 16)
                        self.sD.n += 16
                        e.wait_ge(self.sD.h, self.sD.n)
                    blk.sync(ld)

                    def cast(e):
                        e.wait_ge(self.sD.h, self.sD.n)
                        self.sV.inc(e.tensor_copy(xbf[:, :], xw[:, :]))
                    if upto > 0: blk.vector(cast)

                    def sq(e):
                        e.wait_ge(self.sV.h, self.sV.n)
                        self.sA.inc(e.activation(x2bf[:, :], xbf[:, :], AF.Square))
                    if upto > 1: blk.scalar(sq)

                    def stats(e):
                        e.wait_ge(self.sA.h, self.sA.n)
                        for c in range(NCH):
                            e.matmul(lps[0:1, 0:LW], self.ones_k[:, :],
                                     xbf[:, c * LW:(c + 1) * LW],
                                     start=(c == 0), stop=(c == NCH - 1))
                        for c in range(NCH):
                            ins_ = e.matmul(lps[0:1, LW:2 * LW], self.ones_k[:, :],
                                            x2bf[:, c * LW:(c + 1) * LW],
                                            start=(c == 0), stop=(c == NCH - 1))
                        self.sP.inc(ins_)
                    if upto > 2: blk.tensor(stats)

                    def astats(e):
                        e.wait_ge(self.sP.h, self.sP.n)
                        e.activation(st2[0:1, 0:LW], lps[0:1, 0:LW], AF.Copy, scale=1.0 / D)
                        self.sA.inc(e.activation(st2[0:1, LW:2 * LW], lps[0:1, LW:2 * LW],
                                                  AF.Copy, scale=1.0 / D))
                    if upto > 3: blk.scalar(astats)

                    def vstats(e):
                        e.wait_ge(self.sA.h, self.sA.n)
                        e.tensor_tensor(st2[0:1, 2 * LW:3 * LW], st2[0:1, 0:LW],
                                        st2[0:1, 0:LW], ALU.mult)
                        e.drain()
                        self.sV.inc(e.tensor_tensor(st2[0:1, LW:2 * LW], st2[0:1, LW:2 * LW],
                                                     st2[0:1, 2 * LW:3 * LW], ALU.subtract))
                    if upto > 3: blk.vector(vstats)

                    def rstd(e):
                        e.wait_ge(self.sV.h, self.sV.n)
                        e.activation(st2[0:1, 2 * LW:3 * LW], st2[0:1, LW:2 * LW],
                                     AF.Ln, bias=self.eps5[0:1, :])
                        e.drain()
                        self.sA.inc(e.activation(st2[0:1, 2 * LW:3 * LW],
                                                  st2[0:1, 2 * LW:3 * LW], AF.Exp, scale=-0.5))
                    if upto > 4: blk.scalar(rstd)

                    def vb(e):
                        e.wait_ge(self.sA.h, self.sA.n)
                        self.sV.inc(e.scalar_tensor_tensor(st2[0:1, 3 * LW:4 * LW],
                                                            st2[0:1, 0:LW], -1.0,
                                                            st2[0:1, 2 * LW:3 * LW],
                                                            ALU.mult, ALU.mult))
                    if upto > 5: blk.vector(vb)

                    def bcast(e):
                        e.wait_ge(self.sV.h, self.sV.n)
                        e.matmul(lpb[:, 0:LW], self.ones_mf[:, :],
                                 st2[0:1, 2 * LW:3 * LW], start=True, stop=True)
                        self.sP.inc(e.matmul(lpb[:, LW:2 * LW], self.ones_mf[:, :],
                                              st2[0:1, 3 * LW:4 * LW], start=True, stop=True))
                    if upto > 6: blk.tensor(bcast)

                    def vxn(e):
                        e.wait_ge(self.sP.h, self.sP.n)
                        for c in range(NCH):
                            cs = slice(c * LW, (c + 1) * LW)
                            e.tensor_tensor(xnf[:, cs], lpb[:, 0:LW], xw[:, cs], ALU.mult)
                            e.drain()
                            e.tensor_tensor(xnf[:, cs], lpb[:, LW:2 * LW], xnf[:, cs], ALU.add)
                        e.drain()
                        self.sV.inc(e.tensor_copy(xnb[:, :], xnf[:, :]))
                    if upto > 7: blk.vector(vxn)

                    def st(e, w0=w0):
                        e.wait_ge(self.sV.h, self.sV.n)
                        e.dma_start(dch(self.xn_b.ap()[:, w0:w0 + LW]),
                                    sch(xnb[:, :], NCH)).then_inc(self.sO.h, 16)
                        e.dma_start(dch(self.xmA.ap()[:, w0:w0 + LW]),
                                    sch(xnf[:, :], NCH)).then_inc(self.sO.h, 16)
                        self.sO.n += 32
                    if upto > 8: blk.sync(st)

    def p34_outer(self):
        nc = self.nc
        with nc.sbuf_tensor("kT_res", [128, NCH * S], BF16) as kT:
            self._p3_kv(kT)
            self._p4_att(kT)

    def _p3_kv(self, kT):
        nc = self.nc
        with nc.sbuf_tensor("wk_sb", [128, NCH * D], BF16) as wk, \
             nc.sbuf_tensor("wv_sb", [128, NCH * D], BF16) as wv, \
             nc.sbuf_tensor("xnw3", [128, NCH * LW], BF16) as xnw, \
             nc.sbuf_tensor("vw3", [128, D], BF16) as vw, \
             nc.psum_tensor("pk3", [128, 2 * 512], F32) as pk, \
             nc.psum_tensor("pv3", [128, D], F32) as pv:
            with nc.Block() as blk:
                def ld(e):
                    e.wait_ge(self.sO.h, self.sO.n)
                    e.dma_start(sch(wk[:, :], NCH), dch(self.wkT_b.ap())).then_inc(self.sD.h, 16)
                    e.dma_start(sch(wv[:, :], NCH), dch(self.wvT_b.ap())).then_inc(self.sD.h, 16)
                    self.sD.n += 32
                blk.sync(ld)
            for w in range(S // LW):
                w0 = w * LW
                with nc.Block() as blk:
                    def ld(e, w0=w0):
                        e.dma_start(sch(xnw[:, :], NCH),
                                    dch(self.xn_b.ap()[:, w0:w0 + LW])).then_inc(self.sD.h, 16)
                        self.sD.n += 16
                    blk.sync(ld)

                    evac_cnt = [self.sA.n]
                    for ec in range(NCH):
                        bank = ec % 2

                        def pe(e, ec=ec, bank=bank):
                            if ec == 0:
                                e.wait_ge(self.sD.h, self.sD.n)
                            if ec >= 2:
                                e.wait_ge(self.sA.h, evac_cnt[0] + ec - 1)
                            for dc in range(NCH):
                                i_ = e.matmul(pk[:, bank * 512:bank * 512 + 512],
                                              wk[:, dc * D + ec * 128:dc * D + (ec + 1) * 128],
                                              xnw[:, dc * LW:(dc + 1) * LW],
                                              start=(dc == 0), stop=(dc == NCH - 1))
                            self.sP.inc(i_)
                        blk.tensor(pe)

                        def ac(e, ec=ec, bank=bank, w0=w0):
                            e.wait_ge(self.sP.h, self.sP.n)
                            self.sA.inc(e.activation(
                                kT[:, ec * S + w0:ec * S + w0 + LW],
                                pk[:, bank * 512:bank * 512 + 512], AF.Identity,
                                bias=self.pvec[:, PV_BK + ec:PV_BK + ec + 1]))
                        blk.scalar(ac)

                    for t in range(LW // 128):
                        def pe(e, t=t):
                            e.wait_ge(self.sA.h, self.sA.n)
                            for half in range(2):
                                hs = slice(half * 512, half * 512 + 512)
                                for dc in range(NCH):
                                    e.matmul(pv[:, hs],
                                             xnw[:, dc * LW + t * 128:dc * LW + (t + 1) * 128],
                                             wv[:, dc * D + half * 512:dc * D + half * 512 + 512],
                                             start=(dc == 0), stop=False)
                                i_ = e.matmul(pv[:, hs], self.ones_m[:, :],
                                              self.brow_bf[0:1, hs], start=False, stop=True)
                            self.sP.inc(i_)
                        blk.tensor(pe)

                        def ac(e):
                            e.wait_ge(self.sP.h, self.sP.n)
                            e.wait_ge(self.sO.h, self.sO.n)
                            self.sA.inc(e.activation(vw[:, :], pv[:, :], AF.Copy))
                        blk.scalar(ac)

                        def st(e, w0=w0, t=t):
                            e.wait_ge(self.sA.h, self.sA.n)
                            e.dma_start(self.vsc.ap()[w0 + t * 128:w0 + (t + 1) * 128, :],
                                        vw[:, :]).then_inc(self.sO.h, 16)
                            self.sO.n += 16
                        blk.sync(st)

    def _p4_att(self, kT):
        nc = self.nc
        NB = S // LW
        from contextlib import ExitStack
        es4 = ExitStack()
        wq = es4.enter_context(nc.sbuf_tensor("wq_sb", [128, NCH * D], BF16))
        wo = es4.enter_context(nc.sbuf_tensor("wo_sb", [128, NCH * D], BF16))
        xnq = es4.enter_context(nc.sbuf_tensor("xnq", [128, NCH * LW], BF16))
        qT = es4.enter_context(nc.sbuf_tensor("qT4", [128, NCH * LW], BF16))
        theta = es4.enter_context(nc.sbuf_tensor("theta", [128, S], F32))
        thA = es4.enter_context(nc.sbuf_tensor("thA", [128, 512], F32))
        thB = es4.enter_context(nc.sbuf_tensor("thB", [128, 512], F32))
        psb = es4.enter_context(nc.sbuf_tensor("p4p", [128, S], BF16))
        pT4 = es4.enter_context(nc.sbuf_tensor("pT4", [128, 4 * S], BF16))
        att4 = es4.enter_context(nc.sbuf_tensor("att4", [128, 4 * D], BF16))
        aoT = es4.enter_context(nc.sbuf_tensor("aoT4", [128, NCH * LW], BF16))
        vch = es4.enter_context(nc.sbuf_tensor("vch", [128, 2 * D], BF16))
        rmx = es4.enter_context(nc.sbuf_tensor("rmx", [128, 1], F32))
        rsum = es4.enter_context(nc.sbuf_tensor("rsum", [128, 1], F32))
        rcp = es4.enter_context(nc.sbuf_tensor("rcp", [128, 4], F32))
        scp = es4.enter_context(nc.psum_tensor("sc4", [128, 2 * 512], F32))
        trp = es4.enter_context(nc.psum_tensor("tr4", [128, 1024], BF16))
        avp = es4.enter_context(nc.psum_tensor("av4", [128, 2 * D], F32))
        if True:
            with nc.Block() as blk:
                def ld(e):
                    e.wait_ge(self.sO.h, self.sO.n)
                    e.dma_start(sch(wq[:, :], NCH), dch(self.wqT_b.ap())).then_inc(self.sD.h, 16)
                    e.dma_start(sch(wo[:, :], NCH), dch(self.woT_b.ap())).then_inc(self.sD.h, 16)
                    self.sD.n += 32
                blk.sync(ld)
            for b in range(NB):
                b0 = b * LW
                with nc.Block() as blk:
                    def ld(e, b0=b0):
                        e.dma_start(sch(xnq[:, :], NCH),
                                    dch(self.xn_b.ap()[:, b0:b0 + LW])).then_inc(self.sD.h, 16)
                        self.sD.n += 16
                    blk.sync(ld)

                    for ec in range(NCH):
                        bank = ec % 2
                        acn = [0]

                        def pe(e, ec=ec, bank=bank):
                            if ec == 0:
                                e.wait_ge(self.sD.h, self.sD.n)
                                e.wait_ge(self.sO.h, self.sO.n)
                            if ec >= 2:
                                e.wait_ge(self.sA.h, self.sA.n - 1)
                            for dc in range(NCH):
                                i_ = e.matmul(scp[:, bank * 512:bank * 512 + 512],
                                              wq[:, dc * D + ec * 128:dc * D + (ec + 1) * 128],
                                              xnq[:, dc * LW:(dc + 1) * LW],
                                              start=(dc == 0), stop=(dc == NCH - 1))
                            self.sP.inc(i_)
                        blk.tensor(pe)

                        def ac(e, ec=ec, bank=bank):
                            e.wait_ge(self.sP.h, self.sP.n)
                            self.sA.inc(e.activation(
                                qT[:, ec * LW:(ec + 1) * LW],
                                scp[:, bank * 512:bank * 512 + 512], AF.Identity,
                                bias=self.pvec[:, PV_BQ + ec:PV_BQ + ec + 1]))
                        blk.scalar(ac)

                for t in range(4):
                  with nc.Block() as blk:
                    if True:
                        for kc in range(NCH):
                            bank = kc % 2

                            def pe(e, t=t, kc=kc, bank=bank):
                                e.wait_ge(self.sA.h, self.sA.n)
                                if kc >= 2:
                                    e.wait_ge(self.sV.h, self.sV.n - 3 * 1)
                                for dc in range(NCH):
                                    i_ = e.matmul(scp[:, bank * 512:bank * 512 + 512],
                                                  qT[:, dc * LW + t * 128:dc * LW + (t + 1) * 128],
                                                  kT[:, dc * S + kc * 512:dc * S + (kc + 1) * 512],
                                                  start=(dc == 0), stop=(dc == NCH - 1))
                                self.sP.inc(i_)
                            blk.tensor(pe)

                            def ac(e, kc=kc, bank=bank):
                                e.wait_ge(self.sP.h, self.sP.n)
                                e.wait_ge(self.sV.h, self.sV.n)
                                e.activation(thA[:, :], scp[:, bank * 512:bank * 512 + 512],
                                             AF.Tanh, scale=0.5)
                                self.sA.inc(e.activation(
                                    thB[:, :], scp[:, bank * 512:bank * 512 + 512], AF.Tanh))
                            blk.scalar(ac)

                            def ve(e, kc=kc, bank=bank):
                                e.wait_ge(self.sA.h, self.sA.n)
                                cs = slice(kc * 512, (kc + 1) * 512)
                                e.tensor_scalar(theta[:, cs], scp[:, bank * 512:bank * 512 + 512],
                                                0.0, 0.1, ALU.max, ALU.mult)
                                e.drain()
                                e.scalar_tensor_tensor(theta[:, cs], thA[:, :], 0.1,
                                                       theta[:, cs], ALU.mult, ALU.add)
                                e.drain()
                                self.sV.inc(e.scalar_tensor_tensor(
                                    theta[:, cs], thB[:, :], 0.15, theta[:, cs],
                                    ALU.mult, ALU.add), 3)
                            blk.vector(ve)

                        def vm(e):
                            e.drain()
                            self.sV.inc(e.reduce_max(rmx[:, :], theta[:, :], AX.X, negate=True))
                        blk.vector(vm)

                        def ax(e):
                            e.wait_ge(self.sV.h, self.sV.n)
                            self.sA.inc(e.activation(psb[:, :], theta[:, :], AF.Exp,
                                                     bias=rmx[:, :]))
                        blk.scalar(ax)

                        def vr(e, t=t, b=b):
                            e.wait_ge(self.sA.h, self.sA.n)
                            e.reduce_sum(rsum[:, :], psb[:, :], AX.X)
                            e.drain()
                            e.reciprocal(rcp[:, t:t + 1], rsum[:, :])
                            if b == 0 and "p4s" in self.debug:
                                e.tensor_copy(self.snap4[:, t * 80:t * 80 + 32], theta[:, 0:32])
                                e.tensor_copy(self.snap4[:, t * 80 + 32:t * 80 + 64], psb[:, 0:32])
                                e.tensor_copy(self.snap4[:, t * 80 + 64:t * 80 + 65], rsum[:, :])
                                e.tensor_copy(self.snap4[:, t * 80 + 65:t * 80 + 66], rmx[:, :])
                                e.drain()
                                e.tensor_copy(self.snap4[:, t * 80 + 66:t * 80 + 67], rcp[:, t:t + 1])
                            self.sV.inc(e.drain())
                        blk.vector(vr)

                        for grp in range(4):
                            def pe(e, t=t, grp=grp):
                                if grp == 0:
                                    e.wait_ge(self.sA.h, self.sA.n)
                                else:
                                    e.wait_ge(self.sV.h, self.sV.n)
                                for j8 in range(8):
                                    j = grp * 8 + j8
                                    i_ = e.matmul(trp[:, j8 * 128:(j8 + 1) * 128],
                                                  psb[:, j * 128:(j + 1) * 128],
                                                  self.ident[:, :], is_transpose=True,
                                                  start=(j8 == 0), stop=(j8 == 7))
                                self.sP.inc(i_)
                            blk.tensor(pe)

                            def vt(e, t=t, grp=grp):
                                e.wait_ge(self.sP.h, self.sP.n)
                                self.sV.inc(e.tensor_copy(
                                    pT4[:, t * S + grp * 1024:t * S + (grp + 1) * 1024],
                                    trp[:, :]))
                            blk.vector(vt)

                for hv in range(2):
                  with nc.Block() as blk:
                    if True:
                        tl0, tl1 = 2 * hv, 2 * hv + 1
                        for tc in range(S // 128):
                            def ldv(e, tc=tc, hv=hv):
                                if tc >= 2 or hv > 0:
                                    e.wait_ge(self.sP.h, self.sP.n)
                                e.dma_start(vch[:, (tc % 2) * D:(tc % 2 + 1) * D],
                                            self.vsc.ap()[tc * 128:(tc + 1) * 128, :]
                                            ).then_inc(self.sD.h, 16)
                                self.sD.n += 16
                            blk.sync(ldv)

                            def pe(e, tc=tc, tl0=tl0, tl1=tl1):
                                e.wait_ge(self.sD.h, self.sD.n)
                                if tc == 0:
                                    e.wait_ge(self.sV.h, self.sV.n)
                                for ti, tl in ((0, tl0), (1, tl1)):
                                    for half in range(2):
                                        i_ = e.matmul(
                                            avp[:, ti * D + half * 512:ti * D + half * 512 + 512],
                                            pT4[:, tl * S + tc * 128:tl * S + (tc + 1) * 128],
                                            vch[:, (tc % 2) * D + half * 512:(tc % 2) * D + half * 512 + 512],
                                            start=(tc == 0), stop=(tc == S // 128 - 1))
                                self.sP.inc(i_)
                            blk.tensor(pe)

                        def vav(e, hv=hv, tl0=tl0, tl1=tl1):
                            e.wait_ge(self.sP.h, self.sP.n)
                            for ti, tl in ((0, tl0), (1, tl1)):
                                i_ = e.tensor_scalar(att4[:, tl * D:(tl + 1) * D],
                                                     avp[:, ti * D:(ti + 1) * D],
                                                     rcp[:, tl:tl + 1], None, ALU.mult)
                            self.sV.inc(i_)
                        blk.vector(vav)

                with nc.Block() as blk:
                    for tl in range(4):
                        def pe(e, tl=tl):
                            e.wait_ge(self.sV.h, self.sV.n)
                            for j in range(8):
                                i_ = e.matmul(trp[:, j * 128:(j + 1) * 128],
                                              att4[:, tl * D + j * 128:tl * D + (j + 1) * 128],
                                              self.ident[:, :], is_transpose=True,
                                              start=(j == 0), stop=(j == 7))
                            self.sP.inc(i_)
                        blk.tensor(pe)

                        def vt(e, tl=tl):
                            e.wait_ge(self.sP.h, self.sP.n)
                            self.sV.inc(e.tensor_copy(
                                sch(aoT[:, :], NCH)[:, :, tl * 128:(tl + 1) * 128],
                                sch(trp[:, :], 8)))
                        blk.vector(vt)

                    for ec in range(NCH):
                        bank = ec % 2

                        def pe(e, ec=ec, bank=bank):
                            e.wait_ge(self.sV.h, self.sV.n)
                            if ec >= 2:
                                e.wait_ge(self.sA.h, self.sA.n - 1)
                            for dc in range(NCH):
                                i_ = e.matmul(scp[:, bank * 512:bank * 512 + 512],
                                              wo[:, dc * D + ec * 128:dc * D + (ec + 1) * 128],
                                              aoT[:, dc * LW:(dc + 1) * LW],
                                              start=(dc == 0), stop=(dc == NCH - 1))
                            self.sP.inc(i_)
                        blk.tensor(pe)

                        def ac(e, ec=ec, bank=bank):
                            e.wait_ge(self.sP.h, self.sP.n)
                            if ec < 2:
                                e.wait_ge(self.sO.h, self.sO.n)
                            self.sA.inc(e.activation(
                                theta[:, ec * LW:(ec + 1) * LW],
                                scp[:, bank * 512:bank * 512 + 512], AF.Identity,
                                bias=self.pvec[:, PV_BO + ec:PV_BO + ec + 1]))
                        blk.scalar(ac)

                    def st(e, b0=b0):
                        e.wait_ge(self.sA.h, self.sA.n)
                        e.dma_start(dch(self.ssc.ap()[:, b0:b0 + LW]),
                                    sch(theta[:, :], NCH)).then_inc(self.sO.h, 16)
                        self.sO.n += 16
                    blk.sync(st)
                    if b == 0 and "p4s" in self.debug:
                        self.tap5(blk, "snap4", self.snap4, [128, 320], F32)
                    if b == 0 and "p4d" in self.debug:
                        self.tap5(blk, "qT4", qT, [128, NCH * LW], BF16)
                        self.tap5(blk, "pT4", pT4, [128, 4 * S], BF16)
                        self.tap5(blk, "att4", att4, [128, 4 * D], BF16)
                        self.tap5(blk, "aoT4", aoT, [128, NCH * LW], BF16)
                        self.tap5(blk, "psb4", psb, [128, S], BF16)
                        self.tap5(blk, "rcp4", rcp, [128, 4], F32)
        es4.close()

    def p5_mini(self, layers=None):
        for l in range(layers if layers is not None else NL):
            self._p5a_attn(l)
            self._p5b_ffn(l)

    def _p5a_attn(self, l):
        nc = self.nc
        from contextlib import ExitStack
        es = ExitStack()
        sb = lambda n_, sh_, dt_: es.enter_context(nc.sbuf_tensor(n_ + f"_{l}a", sh_, dt_))
        wqkv = sb("wqkv5", [128, NCH * 3 * D], BF16)
        wom = sb("wom5", [128, NCH * D], BF16)
        x_win = sb("xwin5", [128, NCH * 513], F32)
        x_sh = sb("xsh5", [128, NCH * WA], BF16)
        qtok = sb("qtok5", [128, 4 * D], BF16)
        ktok = sb("ktok5", [128, D], BF16)
        v_roll = sb("vroll5", [128, 6 * D], BF16)
        kT_roll = sb("ktr5", [128, NCH * 768], BF16)
        qTw = sb("qtw5", [128, NCH * WA], BF16)
        pT_h = sb("pth5", [128, 6 * WA], BF16)
        aoT = sb("aot5", [128, NCH * WA], BF16)
        tabs = sb("tabs5", [128, 4 * 4 * DH], F32)
        teff = sb("teff5", [128, 4 * 4 * DH], F32)
        qsb4 = sb("qsb5", [128, 4 * DH], F32)
        rotA = sb("rota5", [128, D], F32)
        rotB = sb("rotb5", [128, D], F32)
        sqb = sb("sqb5", [128, D], F32)
        stats = sb("stats5", [128, 4 * HEADS], F32)
        kst = sb("kst5", [128, HEADS], F32)
        sst = sb("sst5", [128, 2 * HEADS], F32)
        p5p = sb("p5p", [128, 4 * 384], BF16)
        tmsk = sb("tmsk5", [128, 4 * 384], F32)
        mask4 = sb("mask45", [128, 4 * 384], F32)
        rs5 = sb("rs5", [128, 4], F32)
        rc5 = sb("rc5", [128, 4], F32)
        qrow = sb("qrow5", [1, 2 * DH], F32)
        dbs = sb("dbs5", [128, 1536], F32)
        pp = es.enter_context(nc.psum_tensor(f"pp5_{l}a", [128, 3 * D], F32))
        ptr = es.enter_context(nc.psum_tensor(f"ptr5_{l}a", [128, 1024], BF16))
        pbc = es.enter_context(nc.psum_tensor(f"pbc5_{l}a", [128, 512], F32))

        TQ1, TQ2, TK1, TK2 = 0, 1, 2, 3

        def tab(buf, i, t=None):
            base = i * 4 * DH
            if t is None:
                return sch(buf[:, base:base + 4 * DH], 4)
            return buf[:, base + t * DH:base + (t + 1) * DH]

        with nc.Block() as blk:
            def ld(e):
                e.wait_ge(self.sO.h, self.sO.n)
                e.dma_start(sch(wqkv[:, :], NCH), dch(self.wqkv_b.ap()[l])).then_inc(self.sD.h, 16)
                e.dma_start(sch(wom[:, :], NCH), dch(self.wom_b.ap()[l])).then_inc(self.sD.h, 16)
                e.dma_start(qrow[:, :], self.qsks_d.ap()[l:l + 1, :]).then_inc(self.sD.h, 16)
                self.sD.n += 48
            blk.sync(ld)

            def vz(e):
                e.wait_ge(self.sD.h, self.sD.n)
                e.memset(kT_roll[:, :], 0.0)
                e.memset(v_roll[:, :], 0.0)
                self.sV.inc(e.memset(pT_h[:, :], 0.0))
            blk.vector(vz)

            def pe(e):
                e.wait_ge(self.sD.h, self.sD.n)
                e.matmul(pbc[:, 0:DH], self.ones_mf[:, :], qrow[0:1, 0:DH],
                         start=True, stop=True)
                self.sP.inc(e.matmul(pbc[:, DH:2 * DH], self.ones_mf[:, :],
                                     qrow[0:1, DH:2 * DH], start=True, stop=True))
            blk.tensor(pe)

            def ac(e):
                e.wait_ge(self.sP.h, self.sP.n)
                e.activation(qsb4[:, 0:DH], pbc[:, 0:DH], AF.Copy)
                self.sA.inc(e.activation(qsb4[:, 2 * DH:3 * DH], pbc[:, DH:2 * DH], AF.Copy))
            blk.scalar(ac)

            def vr(e):
                e.wait_ge(self.sA.h, self.sA.n)
                e.tensor_copy(qsb4[:, DH:DH + 32], qsb4[:, 32:64])
                e.tensor_copy(qsb4[:, DH + 32:2 * DH], qsb4[:, 0:32])
                e.tensor_copy(qsb4[:, 3 * DH:3 * DH + 32], qsb4[:, 2 * DH + 32:3 * DH])
                self.sV.inc(e.tensor_copy(qsb4[:, 3 * DH + 32:4 * DH],
                                          qsb4[:, 2 * DH:2 * DH + 32]))
            blk.vector(vr)

        for w in range(S // WA):
            w0 = w * WA
            with nc.Block() as blk:
                def ld(e, w=w, w0=w0):
                    if w == 0:
                        e.dma_start(sch(x_win[:, :], NCH)[:, :, 1:513],
                                    dch(self.xmA.ap()[:, 0:WA])).then_inc(self.sD.h, 16)
                    else:
                        e.dma_start(sch(x_win[:, :], NCH),
                                    dch(self.xmA.ap()[:, w0 - 1:w0 + WA])).then_inc(self.sD.h, 16)
                    for i, td in enumerate((self.t1q_d, self.t2q_d, self.t1k_d, self.t2k_d)):
                        e.dma_start(sch(tabs[:, :], 16)[:, 4 * i:4 * i + 4, :],
                                    td.ap()[w0:w0 + WA, :].rearrange("(t p) d -> p t d", p=128)
                                    ).then_inc(self.sD.h, 16)
                    self.sD.n += 16 * 5
                blk.sync(ld)

                def vprep(e, w=w):
                    e.wait_ge(self.sD.h, self.sD.n)
                    if w == 0:
                        e.memset(sch(x_win[:, :], NCH)[:, :, 0:1], 0.0)
                    e.tensor_copy(sch(kT_roll[:, :], NCH)[:, :, 0:256],
                                  sch(kT_roll[:, :], NCH)[:, :, 512:768])
                    e.tensor_copy(sch(x_sh[:, :], NCH)[:, 0:4, :],
                                  sch(x_win[:, :], NCH)[:, 0:4, 1:513])
                    e.tensor_copy(sch(x_sh[:, :], NCH)[:, 4:NCH, :],
                                  sch(x_win[:, :], NCH)[:, 4:NCH, 0:512])
                    for t_ in range(4):
                        g_ = w * 4 + t_
                        v_ = 0 if g_ == 0 else (1 if g_ == 1 else 2)
                        e.tensor_copy(mask4[:, t_ * 384:(t_ + 1) * 384],
                                      self.masks[:, v_ * 384:(v_ + 1) * 384])
                    for i, sc_ in ((TQ1, 0), (TQ2, 1), (TK1, 2), (TK2, 3)):
                        i_ = e.tensor_tensor(
                            tab(teff, i), tab(tabs, i),
                            qsb4[:, sc_ * DH:(sc_ + 1) * DH].unsqueeze(1).broadcast_to([128, 4, DH]),
                            ALU.mult)
                    self.sV.inc(i_)
                blk.vector(vprep)

                if w == 0 and "m5" in self.debug:
                    self.tap5(blk, "negc", self.negc128, [128, NL], F32)
                    self.tap5(blk, "xsh", x_sh, [128, NCH * WA], BF16)
                    self.tap5(blk, "teff", teff, [128, 16 * DH], F32)
                    self.tap5(blk, "qsb4", qsb4, [128, 4 * DH], F32)

                for t in range(4):
                    g = w * 4 + t
                    slot = g % 6

                    def pe(e, t=t):
                        e.wait_ge(self.sV.h, self.sV.n)
                        if t > 0:
                            e.wait_ge(self.sA.h, self.sA.n)
                        for dc in range(NCH):
                            for ee in range(6):
                                i_ = e.matmul(pp[:, ee * 512:(ee + 1) * 512],
                                              x_sh[:, dc * WA + t * 128:dc * WA + (t + 1) * 128],
                                              wqkv[:, dc * 3 * D + ee * 512:dc * 3 * D + (ee + 1) * 512],
                                              start=(dc == 0), stop=(dc == NCH - 1))
                        self.sP.inc(i_)
                    blk.tensor(pe)

                    def asq(e):
                        e.wait_ge(self.sP.h, self.sP.n)
                        self.sA.inc(e.activation(sqb[:, :], pp[:, 0:D], AF.Square))
                    blk.scalar(asq)

                    def vss(e):
                        e.wait_ge(self.sA.h, self.sA.n)
                        self.sV.inc(e.reduce_sum(sst[:, 0:HEADS], sch(sqb[:, :], HEADS), AX.X))
                    blk.vector(vss)

                    def asq2(e):
                        e.wait_ge(self.sV.h, self.sV.n)
                        self.sA.inc(e.activation(sqb[:, :], pp[:, D:2 * D], AF.Square))
                    blk.scalar(asq2)

                    def vss2(e):
                        e.wait_ge(self.sA.h, self.sA.n)
                        self.sV.inc(e.reduce_sum(sst[:, HEADS:2 * HEADS],
                                                 sch(sqb[:, :], HEADS), AX.X))
                    blk.vector(vss2)

                    def arst(e, t=t):
                        e.wait_ge(self.sV.h, self.sV.n)
                        e.activation(sst[:, 0:HEADS], sst[:, 0:HEADS], AF.Ln,
                                     bias=self.eps24[:, :])
                        e.drain()
                        e.activation(stats[:, t * HEADS:(t + 1) * HEADS], sst[:, 0:HEADS],
                                     AF.Exp, scale=-0.5)
                        e.activation(sst[:, HEADS:2 * HEADS], sst[:, HEADS:2 * HEADS],
                                     AF.Ln, bias=self.eps24[:, :])
                        e.drain()
                        self.sA.inc(e.activation(kst[:, :], sst[:, HEADS:2 * HEADS],
                                                 AF.Exp, scale=-0.5))
                    blk.scalar(arst)

                    def vsc8(e, t=t):
                        e.wait_ge(self.sA.h, self.sA.n)
                        self.sV.inc(e.tensor_scalar(stats[:, t * HEADS:(t + 1) * HEADS],
                                                    stats[:, t * HEADS:(t + 1) * HEADS],
                                                    8.0, None, ALU.mult))
                    blk.vector(vsc8)

                    def vrotq(e, t=t):
                        qv = sch(pp[:, 0:D], HEADS)
                        e.tensor_tensor(sch(rotA[:, :], HEADS), qv,
                                        tab(teff, TQ1, t).unsqueeze(1).broadcast_to([128, HEADS, DH]),
                                        ALU.mult)
                        e.tensor_tensor(sch(rotB[:, :], HEADS)[:, :, 0:32], qv[:, :, 32:64],
                                        tab(teff, TQ2, t)[:, 0:32].unsqueeze(1).broadcast_to([128, HEADS, 32]),
                                        ALU.mult)
                        e.tensor_tensor(sch(rotB[:, :], HEADS)[:, :, 32:64], qv[:, :, 0:32],
                                        tab(teff, TQ2, t)[:, 32:64].unsqueeze(1).broadcast_to([128, HEADS, 32]),
                                        ALU.mult)
                        e.drain()
                        self.sV.inc(e.tensor_tensor(qtok[:, t * D:(t + 1) * D],
                                                    rotA[:, :], rotB[:, :], ALU.add))
                    blk.vector(vrotq)

                    def vrotk(e, t=t):
                        kv = sch(pp[:, D:2 * D], HEADS)
                        e.tensor_tensor(sch(rotA[:, :], HEADS), kv,
                                        tab(teff, TK1, t).unsqueeze(1).broadcast_to([128, HEADS, DH]),
                                        ALU.mult)
                        e.tensor_tensor(sch(rotB[:, :], HEADS)[:, :, 0:32], kv[:, :, 32:64],
                                        tab(teff, TK2, t)[:, 0:32].unsqueeze(1).broadcast_to([128, HEADS, 32]),
                                        ALU.mult)
                        e.tensor_tensor(sch(rotB[:, :], HEADS)[:, :, 32:64], kv[:, :, 0:32],
                                        tab(teff, TK2, t)[:, 32:64].unsqueeze(1).broadcast_to([128, HEADS, 32]),
                                        ALU.mult)
                        e.drain()
                        e.tensor_tensor(rotA[:, :], rotA[:, :], rotB[:, :], ALU.add)
                        e.drain()
                        self.sV.inc(e.tensor_tensor(
                            sch(ktok[:, :], HEADS), sch(rotA[:, :], HEADS),
                            kst[:, :].unsqueeze(2).broadcast_to([128, HEADS, DH]), ALU.mult))
                    blk.vector(vrotk)

                    def av5(e, slot=slot):
                        e.wait_ge(self.sV.h, self.sV.n)
                        self.sA.inc(e.activation(v_roll[:, slot * D:(slot + 1) * D],
                                                 pp[:, 2 * D:3 * D], AF.Copy))
                    blk.scalar(av5)

                    for half, (srcbuf, soff) in enumerate(((qtok, t * D), (ktok, 0))):
                        def pe2(e, half=half, srcbuf=srcbuf, soff=soff):
                            e.wait_ge(self.sV.h, self.sV.n)
                            if half == 0 and t == 0:
                                pass
                            for j in range(NCH):
                                i_ = e.matmul(ptr[:, j * 128:(j + 1) * 128],
                                              srcbuf[:, soff + j * 128:soff + (j + 1) * 128],
                                              self.ident[:, :], is_transpose=True,
                                              start=(j == 0), stop=(j == NCH - 1))
                            self.sP.inc(i_)
                        blk.tensor(pe2)

                        def vt2(e, half=half, t=t):
                            e.wait_ge(self.sP.h, self.sP.n)
                            if half == 0:
                                dst = sch(qTw[:, :], NCH)[:, :, t * 128:(t + 1) * 128]
                            else:
                                dst = sch(kT_roll[:, :], NCH)[:, :, 256 + t * 128:256 + (t + 1) * 128]
                            self.sV.inc(e.tensor_copy(dst, sch(ptr[:, :], NCH)))
                        blk.vector(vt2)

                if w == 0 and "m5" in self.debug:
                    self.tap5(blk, "qtok", qtok, [128, 4 * D], BF16)
                    self.tap5(blk, "vroll", v_roll, [128, 6 * D], BF16)
                    self.tap5(blk, "ktr", kT_roll, [128, NCH * 768], BF16)
                    self.tap5(blk, "qtw", qTw, [128, NCH * WA], BF16)
                    self.tap5(blk, "stats", stats, [128, 4 * HEADS], F32)

                SIMB = (0, 512, 1536, 2048)
                for hp in range(HEADS // 2):
                    for sub in range(2):
                        h = 2 * hp + sub
                        pbase = 64 * sub

                        def pe_s(e, h=h, pbase=pbase, hp=hp):
                            e.wait_ge(self.sV.h, self.sV.n)
                            i_ = None
                            for t in range(4):
                                i_ = e.matmul(
                                    pp[:, SIMB[t]:SIMB[t] + 384],
                                    qTw[pbase:pbase + 64, (h // 2) * WA + t * 128:(h // 2) * WA + (t + 1) * 128],
                                    kT_roll[pbase:pbase + 64, (h // 2) * 768 + t * 128:(h // 2) * 768 + t * 128 + 384],
                                    start=True, stop=True)
                            self.sP.inc(i_)
                        blk.tensor(pe_s)

                        def vstt(e, h=h):
                            e.wait_ge(self.sP.h, self.sP.n)
                            e.wait_ge(self.sA.h, self.sA.n)
                            i_ = None
                            for t in range(4):
                                i_ = e.scalar_tensor_tensor(
                                    tmsk[:, t * 384:(t + 1) * 384],
                                    pp[:, SIMB[t]:SIMB[t] + 384],
                                    stats[:, t * HEADS + h:t * HEADS + h + 1],
                                    mask4[:, t * 384:(t + 1) * 384],
                                    ALU.mult, ALU.add)
                            self.sV.inc(i_)
                        blk.vector(vstt)

                        def aex(e, h=h):
                            e.wait_ge(self.sV.h, self.sV.n)
                            self.sA.inc(e.activation(p5p[:, :], tmsk[:, :], AF.Exp,
                                                     bias=self.negc128[:, l:l + 1]))
                        blk.scalar(aex)

                        def vnm(e):
                            e.wait_ge(self.sA.h, self.sA.n)
                            e.reduce_sum(rs5[:, :], sch(p5p[:, :], 4), AX.X)
                            e.drain()
                            e.reciprocal(rc5[:, :], rs5[:, :])
                            e.drain()
                            i_ = None
                            for t in range(4):
                                i_ = e.tensor_scalar(p5p[:, t * 384:(t + 1) * 384],
                                                     p5p[:, t * 384:(t + 1) * 384],
                                                     rc5[:, t:t + 1], None, ALU.mult)
                            self.sV.inc(i_)
                        blk.vector(vnm)

                        for grp in range(2):
                            def pe3(e, grp=grp):
                                e.wait_ge(self.sV.h, self.sV.n)
                                i_ = None
                                for j in range(6):
                                    i_ = e.matmul(ptr[:, j * 128:(j + 1) * 128],
                                                  p5p[:, grp * 768 + j * 128:grp * 768 + (j + 1) * 128],
                                                  self.ident[:, :], is_transpose=True,
                                                  start=(j == 0), stop=(j == 5))
                                self.sP.inc(i_)
                            blk.tensor(pe3)

                            def vt3(e, grp=grp):
                                e.wait_ge(self.sP.h, self.sP.n)
                                i_ = None
                                for tt in range(2):
                                    t = grp * 2 + tt
                                    dst = sch(pT_h[:, :], 6)[:, t:t + 3, t * 128:(t + 1) * 128]
                                    i_ = e.tensor_copy(dst, sch(ptr[:, tt * 384:tt * 384 + 384], 3))
                                self.sV.inc(i_)
                            blk.vector(vt3)

                        def pe4(e, h=h, w=w, pbase=pbase):
                            e.wait_ge(self.sV.h, self.sV.n)
                            g0 = w * 4 - 2
                            for kc in range(6):
                                slot_ = (g0 + kc) % 6
                                i_ = e.matmul(pp[pbase:pbase + 64, 1024:1536],
                                              v_roll[:, slot_ * D + h * DH:slot_ * D + (h + 1) * DH],
                                              pT_h[:, kc * WA:(kc + 1) * WA],
                                              start=(kc == 0), stop=(kc == 5),
                                              tile_position=(0, pbase))
                            self.sP.inc(i_)
                        blk.tensor(pe4)

                    def avp(e, hp=hp):
                        e.wait_ge(self.sP.h, self.sP.n)
                        self.sA.inc(e.activation(aoT[:, hp * WA:(hp + 1) * WA],
                                                 pp[:, 1024:1536], AF.Copy))
                    blk.scalar(avp)

                if w == 0 and "m5" in self.debug:
                    self.tap5(blk, "aot", aoT, [128, NCH * WA], BF16)
                    self.tap5(blk, "pth", pT_h, [128, 6 * WA], BF16)
                    self.tap5(blk, "dbs", dbs, [128, 1536], F32)

                for ec in range(NCH):
                    bank = ec % 2

                    def pe(e, ec=ec, bank=bank):
                        e.wait_ge(self.sA.h, self.sA.n)
                        if ec >= 2:
                            e.wait_ge(self.sV.h, self.sV.n)
                        for dc in range(NCH):
                            i_ = e.matmul(pp[:, 1536 + bank * 512:1536 + bank * 512 + 512],
                                          wom[:, dc * D + ec * 128:dc * D + (ec + 1) * 128],
                                          aoT[:, dc * WA:(dc + 1) * WA],
                                          start=(dc == 0), stop=(dc == NCH - 1))
                        self.sP.inc(i_)
                    blk.tensor(pe)

                    def vres(e, ec=ec, bank=bank):
                        e.wait_ge(self.sP.h, self.sP.n)
                        if ec < 2:
                            e.wait_ge(self.sO.h, self.sO.n)
                        self.sV.inc(e.scalar_tensor_tensor(
                            x_win[:, ec * 513 + 1:ec * 513 + 513],
                            pp[:, 1536 + bank * 512:1536 + bank * 512 + 512], 1.0,
                            x_win[:, ec * 513 + 1:ec * 513 + 513], ALU.mult, ALU.add))
                    blk.vector(vres)

                def st(e, w0=w0):
                    e.wait_ge(self.sV.h, self.sV.n)
                    e.dma_start(dch(self.xmB.ap()[:, w0:w0 + WA]),
                                sch(x_win[:, :], NCH)[:, :, 1:513]).then_inc(self.sO.h, 16)
                    self.sO.n += 16
                blk.sync(st)
        es.close()

    def _p5b_ffn(self, l):
        nc = self.nc
        pvl = PV_L0 + l * PV_PER_L
        from contextlib import ExitStack
        es = ExitStack()
        sb = lambda n_, sh_, dt_: es.enter_context(nc.sbuf_tensor(n_ + f"_{l}b", sh_, dt_))
        x_win = sb("xwinf", [128, NCH * 1025], F32)
        xsh_b = sb("xshb", [128, NCH * LW], BF16)
        x2b = sb("x2b", [128, NCH * LW], BF16)
        xn_bf = sb("xnbf", [128, NCH * WF], BF16)
        xtmp = sb("xtmpf", [128, WF], F32)
        gated = sb("gatedf", [128, NPAIR * WF], BF16)
        gsb = sb("gsbf", [128, WF], BF16)
        w1r = sb("w1rf", [128, 2 * NCH * 256], BF16)
        w2r = sb("w2rf", [128, 2 * NPAIR * 128], BF16)
        st5 = sb("st5f", [1, 3 * LW], F32)
        arow = sb("arowf", [1, S], F32)
        brow5 = sb("brow5f", [1, S], F32)
        pf = es.enter_context(nc.psum_tensor(f"pf_{l}b", [128, 3 * 1024], F32))
        pst = es.enter_context(nc.psum_tensor(f"pst_{l}b", [1, 1024], F32))

        for w in range(S // LW):
            w0 = w * LW
            with nc.Block() as blk:
                def ld(e, w=w, w0=w0):
                    if w == 0:
                        e.wait_ge(self.sO.h, self.sO.n)
                        e.dma_start(sch(x_win[:, 0:NCH * 513], NCH)[:, :, 1:513],
                                    dch(self.xmB.ap()[:, 0:LW])).then_inc(self.sD.h, 16)
                    else:
                        e.dma_start(sch(x_win[:, 0:NCH * 513], NCH),
                                    dch(self.xmB.ap()[:, w0 - 1:w0 + LW])).then_inc(self.sD.h, 16)
                    self.sD.n += 16
                blk.sync(ld)

                def vsh(e, w=w):
                    e.wait_ge(self.sD.h, self.sD.n)
                    if w == 0:
                        e.memset(sch(x_win[:, 0:NCH * 513], NCH)[:, :, 0:1], 0.0)
                    e.tensor_copy(sch(xsh_b[:, :], NCH)[:, 0:4, :],
                                  sch(x_win[:, 0:NCH * 513], NCH)[:, 0:4, 1:513])
                    self.sV.inc(e.tensor_copy(sch(xsh_b[:, :], NCH)[:, 4:NCH, :],
                                              sch(x_win[:, 0:NCH * 513], NCH)[:, 4:NCH, 0:512]))
                blk.vector(vsh)

                def pe(e):
                    e.wait_ge(self.sV.h, self.sV.n)
                    i_ = None
                    for c in range(NCH):
                        i_ = e.matmul(pst[0:1, 0:LW], self.ones_k[:, :],
                                      xsh_b[:, c * LW:(c + 1) * LW],
                                      start=(c == 0), stop=(c == NCH - 1))
                    self.sP.inc(i_)
                blk.tensor(pe)

                def ac(e):
                    e.wait_ge(self.sP.h, self.sP.n)
                    e.wait_ge(self.sV.h, self.sV.n)
                    self.sA.inc(e.activation(st5[0:1, 0:LW], pst[0:1, 0:LW], AF.Copy,
                                             scale=-1.0 / D))
                blk.scalar(ac)

                def pe_b(e):
                    e.wait_ge(self.sA.h, self.sA.n)
                    self.sP.inc(e.matmul(pf[:, 0:LW], self.ones_mf[:, :],
                                         st5[0:1, 0:LW], start=True, stop=True))
                blk.tensor(pe_b)

                def vc(e, w=w):
                    e.wait_ge(self.sP.h, self.sP.n)
                    for c in range(NCH):
                        off = c * 513 + (1 if c < 4 else 0)
                        i_ = e.tensor_tensor(xsh_b[:, c * LW:(c + 1) * LW],
                                             pf[:, 0:LW],
                                             x_win[:, off:off + LW], ALU.add)
                    self.sV.inc(i_)
                blk.vector(vc)

                def asq(e):
                    e.wait_ge(self.sV.h, self.sV.n)
                    self.sA.inc(e.activation(x2b[:, :], xsh_b[:, :], AF.Square))
                blk.scalar(asq)

                def pe2s(e):
                    e.wait_ge(self.sA.h, self.sA.n)
                    i_ = None
                    for c in range(NCH):
                        i_ = e.matmul(pst[0:1, LW:2 * LW], self.ones_k[:, :],
                                      x2b[:, c * LW:(c + 1) * LW],
                                      start=(c == 0), stop=(c == NCH - 1))
                    self.sP.inc(i_)
                blk.tensor(pe2s)

                def ac2(e):
                    e.wait_ge(self.sP.h, self.sP.n)
                    self.sA.inc(e.activation(st5[0:1, LW:2 * LW], pst[0:1, LW:2 * LW],
                                             AF.Copy, scale=1.0 / D))
                blk.scalar(ac2)

                def aa(e, w0=w0):
                    e.wait_ge(self.sA.h, self.sA.n)
                    e.drain()
                    e.activation(arow[0:1, w0:w0 + LW], st5[0:1, LW:2 * LW], AF.Ln,
                                 bias=self.eps5[0:1, :])
                    e.drain()
                    self.sA.inc(e.activation(arow[0:1, w0:w0 + LW], arow[0:1, w0:w0 + LW],
                                             AF.Exp, scale=-0.5))
                blk.scalar(aa)

                def vb(e, w0=w0):
                    e.wait_ge(self.sA.h, self.sA.n)
                    self.sV.inc(e.tensor_tensor(
                        brow5[0:1, w0:w0 + LW], st5[0:1, 0:LW],
                        arow[0:1, w0:w0 + LW], ALU.mult))
                blk.vector(vb)

        for w in range(S // WF):
            w0 = w * WF
            with nc.Block() as blk:
                def ld(e, w=w, w0=w0):
                    if w == 0:
                        e.dma_start(sch(x_win[:, :], NCH)[:, :, 1:1025],
                                    dch(self.xmB.ap()[:, 0:WF])).then_inc(self.sD.h, 16)
                    else:
                        e.dma_start(sch(x_win[:, :], NCH),
                                    dch(self.xmB.ap()[:, w0 - 1:w0 + WF])).then_inc(self.sD.h, 16)
                    self.sD.n += 16
                blk.sync(ld)

                def pe(e, w=w, w0=w0):
                    e.wait_ge(self.sD.h, self.sD.n)
                    e.wait_ge(self.sV.h, self.sV.n)
                    for hh in range(WF // 512):
                        e.matmul(pf[:, hh * 512:(hh + 1) * 512], self.ones_mf[:, :],
                                 arow[0:1, w0 + hh * 512:w0 + (hh + 1) * 512],
                                 start=True, stop=True)
                        i_ = e.matmul(pf[:, 1024 + hh * 512:1024 + (hh + 1) * 512],
                                      self.ones_mf[:, :],
                                      brow5[0:1, w0 + hh * 512:w0 + (hh + 1) * 512],
                                      start=True, stop=True)
                    self.sP.inc(i_)
                blk.tensor(pe)

                for c in range(NCH):
                    def vxn(e, c=c, w=w):
                        e.wait_ge(self.sP.h, self.sP.n)
                        e.wait_ge(self.sA.h, self.sA.n)
                        if c == 0:
                            e.memset(sch(x_win[:, :], NCH)[:, :, 0:1], 0.0) if w == 0 else None
                        xs = x_win[:, c * 1025 + (1 if c < 4 else 0):
                                   c * 1025 + (1 if c < 4 else 0) + WF]
                        e.tensor_tensor(xtmp[:, :], pf[:, 0:WF], xs, ALU.mult)
                        e.drain()
                        self.sV.inc(e.tensor_tensor(xtmp[:, :], pf[:, 1024:1024 + WF],
                                                    xtmp[:, :], ALU.add))
                    blk.vector(vxn)

                    def axn(e, c=c):
                        e.wait_ge(self.sV.h, self.sV.n)
                        self.sA.inc(e.activation(
                            xn_bf[:, c * WF:(c + 1) * WF], xtmp[:, :], AF.Identity,
                            scale=self.pvec[:, pvl + PV_LNG + c:pvl + PV_LNG + c + 1],
                            bias=self.pvec[:, pvl + PV_LNB + c:pvl + PV_LNB + c + 1]))
                    blk.scalar(axn)

                if w == 0 and "m5b" in self.debug:
                    self.tap5(blk, "xnbf", xn_bf, [128, NCH * WF], BF16)
                    self.tap5(blk, "arow", arow, [1, S], F32)
                    self.tap5(blk, "brow5", brow5, [1, S], F32)

                for i in range(NPAIR):
                    def ldw(e, i=i, half=0):
                        if i >= 2:
                            e.wait_ge(self.sP.h, self.sP.n)
                        e.dma_start(sch(w1r[:, (i % 2) * NCH * 256:(i % 2 + 1) * NCH * 256], NCH),
                                    dch(self.w1_b.ap()[l][:, 256 * i:256 * (i + 1)])
                                    ).then_inc(self.sD.h, 16)
                        self.sD.n += 16
                    blk.sync(ldw)

                    def pe1(e, i=i):
                        e.wait_ge(self.sD.h, self.sD.n)
                        e.wait_ge(self.sA.h, self.sA.n)
                        if i >= 1:
                            e.wait_ge(self.sV.h, self.sV.n)
                        wb = w1r[:, (i % 2) * NCH * 256:]
                        for half in range(2):
                            for hh in range(WF // 512):
                                dst = pf[:, half * 1024 + hh * 512:half * 1024 + (hh + 1) * 512]
                                for dc in range(NCH):
                                    i_ = e.matmul(dst,
                                                  wb[:, dc * 256 + half * 128:dc * 256 + half * 128 + 128],
                                                  xn_bf[:, dc * WF + hh * 512:dc * WF + (hh + 1) * 512],
                                                  start=(dc == 0), stop=(dc == NCH - 1))
                        self.sP.inc(i_)
                    blk.tensor(pe1)

                    def ag(e, i=i):
                        e.wait_ge(self.sP.h, self.sP.n)
                        e.wait_ge(self.sV.h, self.sV.n)
                        self.sA.inc(e.activation(
                            gsb[:, :], pf[:, 1024:1024 + WF], AF.Gelu,
                            bias=self.pvec[:, pvl + PV_B1 + 2 * i + 1:pvl + PV_B1 + 2 * i + 2]))
                    blk.scalar(ag)

                    def vg(e, i=i):
                        e.wait_ge(self.sA.h, self.sA.n)
                        self.sV.inc(e.scalar_tensor_tensor(
                            gated[:, i * WF:(i + 1) * WF], pf[:, 0:WF],
                            self.pvec[:, pvl + PV_B1 + 2 * i:pvl + PV_B1 + 2 * i + 1],
                            gsb[:, :], ALU.add, ALU.mult))
                    blk.vector(vg)

                if w == 0 and "m5b" in self.debug:
                    self.tap5(blk, "gated", gated, [128, NPAIR * WF], BF16)

                for ec in range(NCH):
                    def ldw2(e, ec=ec):
                        if ec >= 2:
                            e.wait_ge(self.sP.h, self.sP.n)
                        e.dma_start(sch(w2r[:, (ec % 2) * NPAIR * 128:(ec % 2 + 1) * NPAIR * 128], NPAIR),
                                    dch(self.w2_b.ap()[l][:, 128 * ec:128 * (ec + 1)])
                                    ).then_inc(self.sD.h, 16)
                        self.sD.n += 16
                    blk.sync(ldw2)

                    def pe2(e, ec=ec):
                        e.wait_ge(self.sD.h, self.sD.n)
                        e.wait_ge(self.sV.h, self.sV.n)
                        wb = w2r[:, (ec % 2) * NPAIR * 128:]
                        for hh in range(WF // 512):
                            dst = pf[:, 2048 + hh * 512:2048 + (hh + 1) * 512]
                            for ic in range(NPAIR):
                                i_ = e.matmul(dst, wb[:, ic * 128:(ic + 1) * 128],
                                              gated[:, ic * WF + hh * 512:ic * WF + (hh + 1) * 512],
                                              start=(ic == 0), stop=(ic == NPAIR - 1))
                        self.sP.inc(i_)
                    blk.tensor(pe2)

                    def vres(e, ec=ec):
                        e.wait_ge(self.sP.h, self.sP.n)
                        if ec < 2:
                            e.wait_ge(self.sO.h, self.sO.n)
                        self.sV.inc(e.scalar_tensor_tensor(
                            x_win[:, ec * 1025 + 1:ec * 1025 + 1025],
                            pf[:, 2048:2048 + WF],
                            self.pvec[:, pvl + PV_B2 + ec:pvl + PV_B2 + ec + 1],
                            x_win[:, ec * 1025 + 1:ec * 1025 + 1025], ALU.add, ALU.add))
                    blk.vector(vres)

                def st(e, w0=w0):
                    e.wait_ge(self.sV.h, self.sV.n)
                    e.dma_start(dch(self.xmA.ap()[:, w0:w0 + WF]),
                                sch(x_win[:, :], NCH)[:, :, 1:1025]).then_inc(self.sO.h, 16)
                    self.sO.n += 16
                blk.sync(st)
        es.close()

    def p6_final(self):
        nc = self.nc
        from contextlib import ExitStack
        es = ExitStack()
        g_win = es.enter_context(nc.sbuf_tensor("g6", [128, NCH * LW], F32))
        s_win = es.enter_context(nc.sbuf_tensor("s6", [128, NCH * LW], F32))
        xw6 = es.enter_context(nc.sbuf_tensor("xw6", [128, NCH * LW], F32))
        for w in range(S // LW):
            w0 = w * LW
            with nc.Block() as blk:
                def ld(e, w=w, w0=w0):
                    if w == 0:
                        e.wait_ge(self.sO.h, self.sO.n)
                    e.dma_start(sch(g_win[:, :], NCH),
                                dch(self.xmA.ap()[:, w0:w0 + LW])).then_inc(self.sD.h, 16)
                    e.dma_start(sch(s_win[:, :], NCH),
                                dch(self.ssc.ap()[:, w0:w0 + LW])).then_inc(self.sD.h, 16)
                    e.dma_start(sch(xw6[:, :], NCH),
                                dch(self.xT.ap()[:, w0:w0 + LW])).then_inc(self.sD.h, 16)
                    self.sD.n += 48
                blk.sync(ld)

                def ac(e):
                    e.wait_ge(self.sD.h, self.sD.n)
                    self.sA.inc(e.activation(g_win[:, :], g_win[:, :], AF.Silu))
                blk.scalar(ac)

                def ve(e):
                    e.wait_ge(self.sA.h, self.sA.n)
                    e.tensor_tensor(s_win[:, :], s_win[:, :], g_win[:, :], ALU.mult)
                    e.drain()
                    self.sV.inc(e.tensor_tensor(xw6[:, :], xw6[:, :], s_win[:, :], ALU.add))
                blk.vector(ve)

                def st(e, w0=w0):
                    e.wait_ge(self.sV.h, self.sV.n)
                    e.dma_start(dch(self.outT.ap()[:, w0:w0 + LW]),
                                sch(xw6[:, :], NCH)).then_inc(self.sO.h, 16)
                    self.sO.n += 16
                blk.sync(st)
        es.close()

    def finish(self):
        nc = self.nc
        dram_taps = {
            "xn_b": (self.xn_b, [D, S], BF16),
            "xmA": (self.xmA, [D, S], F32),
            "xmB": (self.xmB, [D, S], F32),
            "vsc": (self.vsc, [S, D], BF16),
            "ssc": (self.ssc, [D, S], F32),
            "wqkv_b": (self.wqkv_b, [NL, D, 3 * D], BF16),
            "wvT_b": (self.wvT_b, [D, D], BF16),
            "wkT_b": (self.wkT_b, [D, D], BF16),
        }
        taps = [t for t in self.debug if t in dram_taps]
        if taps:
            with nc.Block() as blk:
                def dump(e):
                    e.wait_ge(self.sO.h, self.sO.n)
                    for t in taps:
                        h, shape, dt = dram_taps[t]
                        o = self.nc.dram_tensor("dbg_" + t, shape, dt, kind="ExternalOutput")
                        ap = h.ap()
                        e.dma_start(o.ap()[:, :] if len(shape) == 2 else o.ap()[:, :, :],
                                    ap[:, :] if len(shape) == 2 else ap[:, :, :]
                                    ).then_inc(self.sO.h, 16)
                        self.sO.n += 16
                blk.sync(dump)
        with nc.Block() as blk:
            def fin(e):
                e.wait_ge(self.sO.h, self.sO.n)
            blk.sync(fin)


def build(debug=(), phases=("p0", "p1", "p2")):
    nc = bass.Bass("TRN2", target_bir_lowering=False)
    kb = KB(nc, debug=debug)
    if "p0" in phases:
        kb.p0_setup()
    if "p1" in phases:
        kb.p1_cast_weights()
    if "p2" in phases:
        kb.p2_ln()
    if "p3" in phases:
        kb.p34_outer()
    if "p5" in phases:
        kb.p5_mini()
    if "p6" in phases:
        kb.p6_final()
    kb.finish()
    nc.finalize()
    return nc, kb


def kernel(**inputs):
    sh = host_shared(inputs)
    x = np.asarray(inputs["L0_attn_x"], np.float32)
    nc, kb = build(phases=("p0", "p1", "p2", "p3", "p5", "p6"))
    in_maps = []
    for c in range(8):
        m = dict(sh)
        m["xT"] = np.ascontiguousarray(x[c].T)
        in_maps.append(m)
    res = run_bass_kernel_spmd(nc, in_maps, core_ids=list(range(8)))
    out = np.stack([np.asarray(r["outT"], np.float32).T for r in res.results], 0)
    return out
